# version 09
# baseline (speedup 1.0000x reference)
import sys

sys.path.insert(0, "/opt/trn_rl_repo")
import contextlib

import numpy as np
import ml_dtypes

import concourse.bass as bass
import concourse.mybir as mybir
from concourse.bass_utils import run_bass_kernel_spmd

F32 = mybir.dt.float32
BF16 = mybir.dt.bfloat16
AF = mybir.ActivationFunctionType
ALU = mybir.AluOpType
AX = mybir.AxisListType

D = 1024
S = 8192
BATCH = 2
NL = 16
TOWN = NL * 128
NPAIR = 8
EPS = 1e-5


class Sem:
    def __init__(self, handle):
        self.h = handle
        self.v = 0


class Prog:
    def __init__(self):
        self.ops = []

    def step(self, eng, builder, waits=(), incs=()):
        waits = tuple((s, v) for s, v in waits)
        incs = tuple(incs)
        for s, n in incs:
            s.v += n

        def fn(e, builder=builder, waits=waits, incs=incs):
            for sem, val in waits:
                if val > 0:
                    e.wait_ge(sem.h, val)
            inst = builder(e)
            for sem, n in incs:
                inst.then_inc(sem.h, n)

        self.ops.append((eng, fn))

    def emit(self, block):
        for name in ("sync", "gpsimd", "vector", "scalar", "tensor"):
            ops = [fn for eng, fn in self.ops if eng == name]
            if not ops:
                continue

            def make(ops):
                def run(e):
                    for fn in ops:
                        fn(e)
                return run

            getattr(block, name)(make(ops))


def build_nc(debug=False, stop_after=None):
    nc = bass.Bass("TRN2", target_bir_lowering=False)

    din = {}
    for name, shape, dt in [
        ("xk", [S, D], F32), ("xo", [TOWN, D], F32),
        ("wq", [D, D], F32), ("wk", [D, D], F32), ("wv", [D, D], F32), ("wo", [D, D], F32),
        ("w1t", [D, 4096], F32), ("w2t", [D, 4096], F32),
        ("h1t", [4096, 4096], F32), ("h2t", [4096, 4096], F32),
        ("w3t", [4096, D], F32),
        ("s1c", [128, 32], F32), ("s2c", [128, 32], F32),
        ("hs1c", [128, 32], F32), ("hs2c", [128, 32], F32),
        ("s3r", [1, D], F32),
        ("maskT", [128, NL * 4 * 128], BF16),
        ("identb", [128, 128], BF16),
        ("ones_c1", [128, 1], BF16),
        ("ones_r128", [1, 128], BF16),
        ("ones_r64f", [1, 64], F32),
        ("ones_r128f", [1, 128], F32),
        ("epsb", [128, 1], F32),
    ]:
        din[name] = nc.dram_tensor(name, shape, dt, kind="ExternalInput").ap()

    out = nc.dram_tensor("out", [TOWN, D], F32, kind="ExternalOutput").ap()
    dbg = {}
    if debug:
        dbg["xnf"] = nc.dram_tensor("dbg_xnf", [S, D], BF16, kind="ExternalOutput").ap()
        dbg["ot"] = nc.dram_tensor("dbg_ot", [128, NPAIR * TOWN], BF16, kind="ExternalOutput").ap()
        dbg["y1"] = nc.dram_tensor("dbg_y1", [TOWN, D], F32, kind="ExternalOutput").ap()
        dbg["A0"] = nc.dram_tensor("dbg_A0", [128, 32 * 512], BF16, kind="ExternalOutput").ap()
        dbg["B0"] = nc.dram_tensor("dbg_B0", [128, 32 * 512], BF16, kind="ExternalOutput").ap()
        dbg["A2"] = nc.dram_tensor("dbg_A2", [128, 32 * 512], BF16, kind="ExternalOutput").ap()
        dbg["F0"] = nc.dram_tensor("dbg_F0", [128, 4 * 1024], F32, kind="ExternalOutput").ap()
        dbg["T0"] = nc.dram_tensor("dbg_T0", [128, 4 * 1024], F32, kind="ExternalOutput").ap()
        dbg["X0"] = nc.dram_tensor("dbg_X0", [128, 4 * 1024], F32, kind="ExternalOutput").ap()
        dbg["xn2"] = nc.dram_tensor("dbg_xn2", [TOWN, D], BF16, kind="ExternalOutput").ap()

    xnf = nc.dram_tensor("xnf", [S, D], BF16, kind="Internal").ap()
    xnod = nc.dram_tensor("xnod", [TOWN, D], BF16, kind="Internal").ap()
    y1d = nc.dram_tensor("y1d", [TOWN, D], F32, kind="Internal").ap()
    xn2d = nc.dram_tensor("xn2d", [TOWN, D], BF16, kind="Internal").ap()
    wc1 = nc.dram_tensor("wc1", [32, 128, 8 * 128], BF16, kind="Internal").ap()
    wc2 = nc.dram_tensor("wc2", [32, 128, 8 * 128], BF16, kind="Internal").ap()
    wch1 = nc.dram_tensor("wch1", [32, 128, 32 * 128], BF16, kind="Internal").ap()
    wch2 = nc.dram_tensor("wch2", [32, 128, 32 * 128], BF16, kind="Internal").ap()
    wc3 = nc.dram_tensor("wc3", [4096, D], BF16, kind="Internal").ap()

    es = contextlib.ExitStack()
    pOTs = contextlib.ExitStack()
    p2s = contextlib.ExitStack()

    def sb(stack, name, shape, dt):
        return stack.enter_context(nc.sbuf_tensor(name, shape, dt))

    s3B = sb(es, "s3B", [128, D], F32)
    ident = sb(es, "ident", [128, 128], BF16)
    ones_c1 = sb(es, "ones_c1s", [128, 1], BF16)
    ones_r128 = sb(es, "ones_r128s", [1, 128], BF16)
    ones_r64f = sb(es, "ones_r64fs", [1, 64], F32)
    ones_r128f = sb(es, "ones_r128fs", [1, 128], F32)
    epsB = sb(es, "epsB", [128, 1], F32)
    scl_s1 = sb(es, "scl_s1", [128, 32], F32)
    scl_s2 = sb(es, "scl_s2", [128, 32], F32)
    scl_h1 = sb(es, "scl_h1", [128, 32], F32)
    scl_h2 = sb(es, "scl_h2", [128, 32], F32)
    xt = [sb(es, f"xt{i}", [128, D], F32) for i in range(4)]
    sqt = [sb(es, f"sqt{i}", [128, D], F32) for i in range(2)]
    s3r_sb = sqt[1][0:1, 0:D]
    xnt = [sb(es, f"xnt{i}", [128, D], BF16) for i in range(4)]
    st_sm = sb(es, "st_sm", [128, 16], F32)
    t_sb = sb(es, "t_sb", [128, D], F32)

    otbuf = sb(pOTs, "otbuf", [128, NPAIR * TOWN], BF16)

    K2 = [sb(p2s, f"K2{i}", [128, S], BF16) for i in range(2)]
    V2 = sb(p2s, "V2", [128, S], BF16)
    Vtm = [sb(p2s, f"Vtm{i}", [128, 64 * 2 * 65], BF16) for i in range(2)]
    Q2 = [sb(p2s, f"Q2_{i}", [128, TOWN], BF16) for i in range(2)]
    maskT = sb(p2s, "maskT_s", [128, NL * 4 * 128], BF16)
    xnstr = [sb(p2s, f"xnstr{i}", [128, 2048], BF16) for i in range(2)]
    wkb = [sb(p2s, f"wkb{i}", [128, 8 * 128], BF16) for i in range(2)]
    wvb = [sb(p2s, f"wvb{i}", [128, 8 * 128], BF16) for i in range(2)]
    wqb = [sb(p2s, f"wqb{i}", [128, 8 * 128], BF16) for i in range(2)]
    PT = [sb(p2s, f"PT{i}", [128, 1024], BF16) for i in range(2)]
    nmB = [sb(p2s, f"nmB{i}", [128, 1], F32) for i in range(2)]
    nmb16 = sb(p2s, "nmb16", [1, 1], BF16)
    sm1 = sb(p2s, "sm1", [1, 8], F32)
    rinv = sqt[0][0:1, 0:1024]
    rB = sqt[1][0:64, 0:1024]
    conv = [sb(p2s, "conv0", [128, 16 * 128], BF16)]

    psA = es.enter_context(nc.psum_tensor("psA", [128, 2048], F32))
    psB = es.enter_context(nc.psum_tensor("psB", [128, 2048], F32))

    qS = Sem(es.enter_context(nc.semaphore("qS")))
    qG = Sem(es.enter_context(nc.semaphore("qG")))
    qA = Sem(es.enter_context(nc.semaphore("qA")))
    pe = Sem(es.enter_context(nc.semaphore("pe")))
    se = Sem(es.enter_context(nc.semaphore("se")))
    dv = Sem(es.enter_context(nc.semaphore("dv")))
    st_out = Sem(es.enter_context(nc.semaphore("st_out")))

    P = Prog()

    for dst, src in [
        (s3r_sb[:], din["s3r"]), (ident[:], din["identb"]),
        (ones_c1[:], din["ones_c1"]), (ones_r128[:], din["ones_r128"]),
        (ones_r64f[:], din["ones_r64f"]), (ones_r128f[:], din["ones_r128f"]),
        (epsB[:], din["epsb"]),
        (scl_s1[:], din["s1c"]), (scl_s2[:], din["s2c"]),
        (scl_h1[:], din["hs1c"]), (scl_h2[:], din["hs2c"]),
        (maskT[:], din["maskT"]),
    ]:
        P.step("sync", lambda e, d=dst, s=src: e.dma_start(out=d, in_=s), incs=[(qS, 16)])
    qs_consts = qS.v

    for vtb in Vtm:
        vt_ones = vtb.ap().rearrange("p (b c) -> p b c", c=65)[:, :, 64:65]
        P.step("vector", lambda e, o=vt_ones: e.memset(o, 1.0), incs=[(dv, 1)])
    dv_vtones = dv.v

    def mk_s3b(e):
        e.matmul(psA[:, 0:512], ones_r128f[:], s3r_sb[0:1, 0:512],
                 start=True, stop=True, skip_group_check=True)
        return e.matmul(psA[:, 512:1024], ones_r128f[:], s3r_sb[0:1, 512:1024],
                        start=True, stop=True, skip_group_check=True)
    P.step("tensor", mk_s3b, waits=[(qS, qs_consts)], incs=[(pe, 1)])
    P.step("vector", lambda e: e.tensor_copy(out=s3B[:], in_=psA[:, 0:1024]),
           waits=[(pe, pe.v)], incs=[(dv, 1)])
    dv_s3b = dv.v

    def _early_exit(tag):
        P.step("gpsimd", lambda e: e.dma_start(out=out[0:128, :], in_=s3B[:]),
               waits=[(dv, dv.v), (se, se.v), (qS, qS.v), (qG, qG.v), (qA, qA.v)],
               incs=[(st_out, 16)])
        P.step("gpsimd", lambda e: e.engine_nop(), waits=[(st_out, st_out.v)])
        for stk in (p2s, pOTs):
            try:
                stk.close()
            except Exception:
                pass
        with es:
            block = es.enter_context(nc.Block())
            P.emit(block)
        return nc

    conv_jobs = []
    for m in range(32):
        conv_jobs.append((din["w1t"][:, m * 128:(m + 1) * 128]
                          .rearrange("(j p) d -> p j d", p=128), wc1[m], 8 * 128))
        conv_jobs.append((din["w2t"][:, m * 128:(m + 1) * 128]
                          .rearrange("(j p) d -> p j d", p=128), wc2[m], 8 * 128))
    for m in range(32):
        for half in range(2):
            conv_jobs.append((din["h1t"][half * 2048:(half + 1) * 2048, m * 128:(m + 1) * 128]
                              .rearrange("(j p) d -> p j d", p=128),
                              wch1[m][:, half * 2048:(half + 1) * 2048], 16 * 128))
            conv_jobs.append((din["h2t"][half * 2048:(half + 1) * 2048, m * 128:(m + 1) * 128]
                              .rearrange("(j p) d -> p j d", p=128),
                              wch2[m][:, half * 2048:(half + 1) * 2048], 16 * 128))
    for j in range(32):
        conv_jobs.append((din["w3t"][j * 128:(j + 1) * 128, :],
                          wc3[j * 128:(j + 1) * 128, :], D))
    cstate = {"i": 0, "done": []}

    def emit_conv(n):
        for _ in range(n):
            i = cstate["i"]
            if i >= len(conv_jobs):
                return
            src, dst, ncol = conv_jobs[i]
            buf = conv[0][:, 0:ncol]
            if len(src.shape) == 3:
                buf = buf.rearrange("p (j d) -> p j d", d=128)
            w = [(qG, cstate["done"][i - 1])] if i >= 1 else []
            P.step("gpsimd", lambda e, b=buf, s=src: e.dma_start(out=b, in_=s),
                   waits=w, incs=[(qG, 16)])
            ldv = qG.v
            P.step("gpsimd", lambda e, b=buf, d=dst: e.dma_start(out=d, in_=b),
                   waits=[(qG, ldv)], incs=[(qG, 16)])
            cstate["done"].append(qG.v)
            cstate["i"] += 1

    def emit_ln(x_ap, sq_ap, c, sq_waits, stats_waits=()):
        P.step("scalar", lambda e: e.activation(out=sq_ap, in_=x_ap, func=AF.Square),
               waits=sq_waits, incs=[(se, 1)])
        se_sq = se.v

        def reds(e):
            e.reduce_sum(out=c[0], in_=x_ap, axis=AX.X)
            return e.reduce_sum(out=c[1], in_=sq_ap, axis=AX.X)
        P.step("vector", reds, waits=list(stats_waits) + [(se, se_sq)], incs=[(dv, 1)])
        dv_stats = dv.v
        P.step("vector", lambda e: e.scalar_tensor_tensor(
            out=c[2], in0=c[0], scalar=1.0 / (D * D), in1=c[0], op0=ALU.mult, op1=ALU.mult),
            waits=[(dv, dv_stats)], incs=[(dv, 1)])
        P.step("vector", lambda e: e.scalar_tensor_tensor(
            out=c[1], in0=c[1], scalar=1.0 / D, in1=c[2], op0=ALU.mult, op1=ALU.subtract),
            waits=[(dv, dv.v)], incs=[(dv, 1)])
        P.step("scalar", lambda e: e.activation(out=c[2], in_=c[1], func=AF.Sqrt, bias=epsB[:]),
               waits=[(dv, dv.v)], incs=[(se, 1)])
        P.step("vector", lambda e: e.reciprocal(out=c[2], in_=c[2]),
               waits=[(se, se.v)], incs=[(dv, 1)])
        P.step("vector", lambda e: e.scalar_tensor_tensor(
            out=c[3], in0=c[0], scalar=-1.0 / D, in1=c[2], op0=ALU.mult, op1=ALU.mult),
            waits=[(dv, dv.v)], incs=[(dv, 1)])
        return dv.v, dv_stats

    ln_jobs = []
    ln_hist = {}
    qa_hist = {}
    lnc = [tuple(st_sm[:, k + 4 * par:k + 4 * par + 1] for k in range(4))
           for par in range(4)]

    def ln_stage_load_A(i):
        src, dst = ln_jobs[i]
        par = i % 4
        lw = [(se, ln_hist[i - 4]["E"])] if i >= 4 else []
        P.step("sync", lambda e, d=xt[par][:], s=src: e.dma_start(out=d, in_=s),
               waits=lw, incs=[(qS, 16)])
        qsv = qS.v
        c = lnc[par]
        aw = [(qS, qsv)]
        if i >= 4:
            aw.append((qA, qa_hist[i - 4]))
        P.step("scalar", lambda e, par=par, c=c: e.activation(
            out=xnt[par][:], in_=xt[par][:], func=AF.Square, accum_out=c[1]),
            waits=aw, incs=[(se, 1)])
        ln_hist[i] = {"A": se.v}

    def ln_stage_BCD(i):
        par = i % 4
        c = lnc[par]
        h = ln_hist[i]
        P.step("vector", lambda e, par=par, c=c: e.reduce_sum(
            out=c[0], in_=xt[par][:], axis=AX.X), waits=[(se, h["A"])], incs=[(dv, 1)])
        P.step("vector", lambda e, c=c: e.scalar_tensor_tensor(
            out=c[2], in0=c[0], scalar=1.0 / (D * D), in1=c[0], op0=ALU.mult, op1=ALU.mult),
            waits=[(dv, dv.v)], incs=[(dv, 1)])
        P.step("vector", lambda e, c=c: e.scalar_tensor_tensor(
            out=c[1], in0=c[1], scalar=1.0 / D, in1=c[2], op0=ALU.mult, op1=ALU.subtract),
            waits=[(dv, dv.v)], incs=[(dv, 1)])
        P.step("scalar", lambda e, c=c: e.activation(
            out=c[2], in_=c[1], func=AF.Sqrt, bias=epsB[:]),
            waits=[(dv, dv.v)], incs=[(se, 1)])
        P.step("vector", lambda e, c=c: e.reciprocal(out=c[2], in_=c[2]),
               waits=[(se, se.v)], incs=[(dv, 1)])
        P.step("vector", lambda e, c=c: e.scalar_tensor_tensor(
            out=c[3], in0=c[0], scalar=-1.0 / D, in1=c[2], op0=ALU.mult, op1=ALU.mult),
            waits=[(dv, dv.v)], incs=[(dv, 1)])
        h["D"] = dv.v

    def ln_stage_E(i):
        src, dst = ln_jobs[i]
        par = i % 4
        c = lnc[par]
        h = ln_hist[i]
        nw = [(dv, h["D"])]
        P.step("scalar", lambda e, par=par, c=c: e.activation(
            out=xnt[par][:], in_=xt[par][:], func=AF.Identity, bias=c[3], scale=c[2]),
            waits=nw, incs=[(se, 1)])
        h["E"] = se.v
        P.step("scalar", lambda e, par=par, d=dst: e.dma_start(out=d, in_=xnt[par][:]),
               waits=[(se, h["E"])], incs=[(qA, 16)])
        qa_hist[i] = qA.v

    pstate = [dict() for _ in range(NPAIR)]
    prev_pair = {"exp_se": 0, "av_pe": 0, "ot_dv": dv_s3b, "nm_pe": 0}
    r4dv = {}
    QGI = [0]
    prep = []

    def drain_prep(n):
        for _ in range(n):
            if not prep:
                return
            prep.pop(0)()
            emit_conv(1)

    def emit_pair_weights(p, wlw):
        wpar = p % 2
        first = [True]
        for wsrc, wdst in ((din["wk"], wkb[wpar]), (din["wv"], wvb[wpar]),
                           (din["wq"], wqb[wpar])):
            for j in range(8):
                P.step("gpsimd", lambda e, s=wsrc[j * 128:(j + 1) * 128, p * 128:(p + 1) * 128],
                       d=wdst[:, j * 128:(j + 1) * 128]: e.dma_start(out=d, in_=s),
                       waits=wlw if first[0] else [], incs=[(qG, 16)])
                first[0] = False
        pstate[p]["qg_w"] = qG.v

    def _hc_buf(j):
        return xnstr[j // 4][:, (j % 4) * 512:(j % 4 + 1) * 512]

    def fill_prep(p, ln_gate=False):
        st = pstate[p]
        st["kmm"] = []
        st["evac_se"] = []
        st["evac_dv"] = []
        kb = K2[p % 2]
        q2b = Q2[p % 2]
        vtb = Vtm[p % 2]
        nmb = nmB[p % 2]
        guard = pstate[p - 1] if p >= 1 else None
        for hh in range(16):
            def loads(hh=hh, st=st, guard=guard):
                for j in range(8):
                    lw = []
                    if j == 0:
                        if st["kmm"]:
                            lw.append((pe, st["kmm"][-1]))
                        elif guard is not None:
                            lw.append((pe, guard["last_qpe"]))
                        if ln_gate:
                            t = 4 * hh + 3
                            lw.append((qA, qa_hist[10 * (t // 8) + t % 8]))
                    P.step("sync", lambda e, b=_hc_buf(j), hh=hh, j=j: e.dma_start(
                        out=b, in_=xnf[hh * 512:(hh + 1) * 512, j * 128:(j + 1) * 128],
                        transpose=True), waits=lw, incs=[(qS, 16)])
                st["qs_hc"] = qS.v

            def mats(hh=hh, st=st, p=p, kb=kb):
                wpar = p % 2

                def mk(e, wpar=wpar):
                    last = None
                    for j in range(8):
                        buf = _hc_buf(j)
                        st_, sp = j == 0, j == 7
                        e.matmul(psB[:, 1024:1536], wkb[wpar][:, j * 128:(j + 1) * 128],
                                 buf, start=st_, stop=sp, skip_group_check=True)
                        last = e.matmul(psB[:, 1536:2048], wvb[wpar][:, j * 128:(j + 1) * 128],
                                        buf, start=st_, stop=sp, skip_group_check=True)
                    return last
                w = [(qS, st["qs_hc"]), (qG, st["qg_w"])]
                if hh == 0:
                    if p > 0:
                        w.append((dv, pstate[p - 1]["dv_nm"]))
                else:
                    w += [(se, st["evac_se"][hh - 1]), (dv, st["evac_dv"][hh - 1])]
                P.step("tensor", mk, waits=w, incs=[(pe, 1)])
                st["kmm"].append(pe.v)
                P.step("scalar", lambda e, hh=hh, kb=kb: e.copy(
                    out=kb[:, hh * 512:(hh + 1) * 512], in_=psB[:, 1024:1536]),
                    waits=[(pe, pe.v)], incs=[(se, 1)])
                st["evac_se"].append(se.v)
                P.step("vector", lambda e, hh=hh: e.tensor_copy(
                    out=V2[:, hh * 512:(hh + 1) * 512], in_=psB[:, 1536:2048]),
                    waits=[(pe, st["kmm"][hh])], incs=[(dv, 1)])
                st["evac_dv"].append(dv.v)
            prep.append(loads)
            prep.append(mats)

        for pp in range(2):
            for j in range(8):
                def qload(pp=pp, j=j, st=st):
                    if j == 0:
                        lw = [(pe, st["kmm"][-1] if pp == 0 else st["qp_pe"])]
                        if ln_gate:
                            lw.append((qA, qa_hist[NJ - 1]))
                    else:
                        lw = [(pe, st["qp_pe"])]
                    buf = xnstr[j % 2][:, 0:1024]
                    P.step("sync", lambda e, b=buf, pp=pp, j=j: e.dma_start(
                        out=b, in_=xnod[pp * 1024:(pp + 1) * 1024, j * 128:(j + 1) * 128],
                        transpose=True), waits=lw, incs=[(qS, 16)])
                    st["qs_q"] = qS.v

                def qmat(pp=pp, j=j, st=st, p=p):
                    wpar = p % 2
                    buf = xnstr[j % 2][:, 0:1024]

                    def mk(e, j=j, buf=buf, wpar=wpar):
                        st_, sp = j == 0, j == 7
                        e.matmul(psB[:, 1024:1536], wqb[wpar][:, j * 128:(j + 1) * 128],
                                 buf[:, 0:512], start=st_, stop=sp, skip_group_check=True)
                        return e.matmul(psB[:, 1536:2048], wqb[wpar][:, j * 128:(j + 1) * 128],
                                        buf[:, 512:1024], start=st_, stop=sp,
                                        skip_group_check=True)
                    w = [(qS, st["qs_q"]), (qG, st["qg_w"])]
                    if j == 0:
                        if pp == 0:
                            w += [(se, st["evac_se"][15]), (dv, st["evac_dv"][15])]
                        else:
                            w.append((se, st["se_qev0"]))
                    P.step("tensor", mk, waits=w, incs=[(pe, 1)])
                    st["qp_pe"] = pe.v
                prep.append(qload)
                prep.append(qmat)

            def qevac(pp=pp, st=st, q2b=q2b):
                qw = [(pe, st["qp_pe"]), (pe, prev_pair["av_pe"])]
                P.step("scalar", lambda e, pp=pp, q2b=q2b: e.activation(
                    out=q2b[:, pp * 1024:(pp + 1) * 1024], in_=psB[:, 1024:2048],
                    func=AF.Copy, scale=0.125), waits=qw, incs=[(se, 1)])
                if pp == 0:
                    st["se_qev0"] = se.v
                else:
                    st["se_q2"] = se.v
                    st["last_qpe"] = st["qp_pe"]
            prep.append(qevac)

        for g in range(16):
            def tp_g(g=g, st=st, vtb=vtb):
                def mk_tp(e, g=g):
                    last = None
                    for jj in range(4):
                        j = g * 4 + jj
                        for h in range(2):
                            last = e.matmul(
                                psB[:, 1024 + h * 512 + jj * 64:
                                    1024 + h * 512 + (jj + 1) * 64],
                                V2[h * 64:(h + 1) * 64, j * 128:(j + 1) * 128],
                                ident[h * 64:(h + 1) * 64, h * 64:(h + 1) * 64],
                                start=True, stop=True, skip_group_check=True)
                    return last
                if g == 0:
                    w = [(se, st["se_q2"]), (pe, prev_pair["av_pe"])]
                else:
                    w = [(dv, st["vtm_dv"])]
                P.step("tensor", mk_tp, waits=w, incs=[(pe, 1)])
                vw = [(pe, pe.v)]
                if g == 0:
                    vw.append((dv, dv_vtones))
                    vw.append((pe, prev_pair["av_pe"]))

                def mk_vcp(e, g=g, vtb=vtb):
                    vt3 = vtb.ap().rearrange("p (b c) -> p b c", c=65)
                    last = None
                    for h in range(2):
                        d = vt3[:, g * 8 + h:(g + 1) * 8:2, 0:64]
                        s = psB[:, 1024 + h * 512:1024 + h * 512 + 256] \
                            .rearrange("p (b c) -> p b c", c=64)
                        last = e.tensor_copy(out=d, in_=s)
                    return last
                P.step("vector", mk_vcp, waits=vw, incs=[(dv, 1)])
                st["vtm_dv"] = dv.v
                if g == 15:
                    st["dv_vtm"] = dv.v
            prep.append(tp_g)

        def d_ksq(st=st, kb=kb):
            P.step("scalar", lambda e, kb=kb: e.activation(
                out=V2[:, 0:S], in_=kb[:], func=AF.Square),
                waits=[(se, st["evac_se"][15]), (dv, st["dv_vtm"])], incs=[(se, 1)])
            st["se_ksq"] = se.v
        prep.append(d_ksq)

        for r in range(8):
            def d_kn(r=r, st=st):
                def mk_kn(e, r=r):
                    e.matmul(psB[0:1, 1024:1536], ones_c1[:],
                             V2[:, r * 1024:r * 1024 + 512],
                             start=True, stop=True, skip_group_check=True)
                    return e.matmul(psB[0:1, 1536:2048], ones_c1[:],
                                    V2[:, r * 1024 + 512:r * 1024 + 1024],
                                    start=True, stop=True, skip_group_check=True)
                w = [(se, st["se_ksq"])]
                w.append((dv, st["dv_vtm"] if r == 0 else st["kn_dv"]))
                P.step("tensor", mk_kn, waits=w, incs=[(pe, 1)])
                P.step("vector", lambda e: e.reduce_max(
                    out=sm1[0:1, 1:2], in_=psB[0:1, 1024:2048], axis=AX.X),
                    waits=[(pe, pe.v)], incs=[(dv, 1)])
                if r == 0:
                    P.step("vector", lambda e: e.tensor_copy(out=sm1[0:1, 0:1],
                                                             in_=sm1[0:1, 1:2]),
                           waits=[(dv, dv.v)], incs=[(dv, 1)])
                else:
                    P.step("vector", lambda e: e.tensor_tensor(
                        out=sm1[0:1, 0:1], in0=sm1[0:1, 0:1], in1=sm1[0:1, 1:2],
                        op=ALU.max), waits=[(dv, dv.v)], incs=[(dv, 1)])
                st["kn_dv"] = dv.v
            prep.append(d_kn)

        def d_qsq(st=st, q2b=q2b):
            P.step("scalar", lambda e, q2b=q2b: e.activation(
                out=V2[:, 0:TOWN], in_=q2b[:], func=AF.Square),
                waits=[(se, st["se_q2"]), (dv, st["kn_dv"])], incs=[(se, 1)])
            st["se_qsq"] = se.v
        prep.append(d_qsq)

        for r in range(2):
            def d_qn(r=r, st=st):
                def mk_qn(e, r=r):
                    e.matmul(psB[0:1, 1024:1536], ones_c1[:],
                             V2[:, r * 1024:r * 1024 + 512],
                             start=True, stop=True, skip_group_check=True)
                    return e.matmul(psB[0:1, 1536:2048], ones_c1[:],
                                    V2[:, r * 1024 + 512:r * 1024 + 1024],
                                    start=True, stop=True, skip_group_check=True)
                w = [(se, st["se_qsq"]), (dv, st["kn_dv"] if r == 0 else st["qn_dv"])]
                P.step("tensor", mk_qn, waits=w, incs=[(pe, 1)])
                P.step("vector", lambda e: e.reduce_max(
                    out=sm1[0:1, 2:3] if r == 0 else sm1[0:1, 6:7],
                    in_=psB[0:1, 1024:2048], axis=AX.X),
                    waits=[(pe, pe.v)], incs=[(dv, 1)])
                if r == 1:
                    P.step("vector", lambda e: e.tensor_tensor(
                        out=sm1[0:1, 2:3], in0=sm1[0:1, 2:3], in1=sm1[0:1, 6:7],
                        op=ALU.max), waits=[(dv, dv.v)], incs=[(dv, 1)])
                st["qn_dv"] = dv.v
            prep.append(d_qn)

        def d_fin(st=st, nmb=nmb):
            P.step("vector", lambda e: e.tensor_tensor(
                out=sm1[0:1, 3:4], in0=sm1[0:1, 2:3], in1=sm1[0:1, 0:1], op=ALU.mult),
                waits=[(dv, st["qn_dv"])], incs=[(dv, 1)])
            dv_m2 = dv.v
            P.step("scalar", lambda e: e.activation(out=sm1[0:1, 4:5], in_=sm1[0:1, 3:4],
                                                    func=AF.Sqrt),
                   waits=[(dv, dv_m2)], incs=[(se, 1)])
            P.step("scalar", lambda e: e.activation(out=sm1[0:1, 5:6], in_=sm1[0:1, 4:5],
                                                    func=AF.Copy, scale=-1.02, bias=-0.5),
                   waits=[(se, se.v)], incs=[(se, 1)])
            nmw = [(se, se.v), (pe, prev_pair["nm_pe"])]
            P.step("scalar", lambda e: e.activation(out=nmb16[:], in_=sm1[0:1, 5:6],
                                                    func=AF.Copy),
                   waits=nmw, incs=[(se, 1)])
            se_nm = se.v
            P.step("tensor", lambda e: e.matmul(psB[0:128, 1024:1025], ones_r128[:],
                                                nmb16[:], start=True, stop=True,
                                                skip_group_check=True),
                   waits=[(se, se_nm), (dv, st["qn_dv"])], incs=[(pe, 1)])
            prev_pair["nm_pe"] = pe.v
            nmcw = [(pe, pe.v), (se, prev_pair["exp_se"])]
            P.step("vector", lambda e, nmb=nmb: e.tensor_copy(out=nmb[:],
                                                              in_=psB[0:128, 1024:1025]),
                   waits=nmcw, incs=[(dv, 1)])
            st["dv_nm"] = dv.v
        prep.append(d_fin)

    def emit_p2f(p):
        st = pstate[p]
        dv_nm = st["dv_nm"]
        dv_vtm = st["dv_vtm"]
        se_q2 = st["se_q2"]
        kb = K2[p % 2]
        q2b = Q2[p % 2]
        vtb = Vtm[p % 2]
        nmb = nmB[p % 2]
        slot_exp_se = []
        slot_av_pe = []
        slot_mm_pe = []
        diag_dv = []
        pend_av = None
        p2g_pend = [None]
        s_idx = 0

        def p2g_head(q, pe_q_last):
            qg = QGI[0]
            stt = {}
            for h in (0, 1):
                P.step("scalar", lambda e, h=h: e.activation(
                    out=rinv[0:1, h * 512:(h + 1) * 512],
                    in_=psB[64:65, h * 512:h * 512 + 512], func=AF.Ln),
                    waits=[(pe, pe_q_last)] if h == 0 else [(se, se.v)], incs=[(se, 1)])
                P.step("scalar", lambda e, h=h: e.activation(
                    out=rinv[0:1, h * 512:(h + 1) * 512],
                    in_=rinv[0:1, h * 512:(h + 1) * 512], func=AF.Exp, scale=-1.0),
                    waits=[(se, se.v)], incs=[(se, 1)])
                stt[h] = se.v
            return (qg, q, stt)

        def p2g_tail(info):
            qg, q, stt = info
            for h in (0, 1):
                P.step("tensor", lambda e, h=h: e.matmul(
                    psB[64:128, h * 512:h * 512 + 512], ones_r64f[:],
                    rinv[0:1, h * 512:(h + 1) * 512],
                    start=True, stop=True, skip_group_check=True),
                    waits=[(se, stt[h])], incs=[(pe, 1)])
                w = [(pe, pe.v)]
                if (qg - 1, h) in r4dv:
                    w.append((dv, r4dv[(qg - 1, h)]))
                P.step("scalar", lambda e, h=h: e.copy(
                    out=rB[:, h * 512:(h + 1) * 512],
                    in_=psB[64:128, h * 512:h * 512 + 512]),
                    waits=w, incs=[(se, 1)])
                sev = se.v
                P.step("vector", lambda e, h=h, q=q: e.tensor_tensor(
                    out=otbuf[h * 64:(h + 1) * 64,
                              p * TOWN + q * 512:p * TOWN + (q + 1) * 512],
                    in0=psB[0:64, h * 512:h * 512 + 512],
                    in1=rB[:, h * 512:(h + 1) * 512], op=ALU.mult),
                    waits=[(se, sev)], incs=[(dv, 1)])
                r4dv[(qg, h)] = dv.v

        for q in range(4):
            qg = QGI[0]
            qlo, qhi = 4 * q, 4 * q + 3
            jmax = 4 * (qhi + 1)
            for j in range(jmax):
                par_s = s_idx % 2
                blo = max(j // 4, qlo)
                pst = psA[:, par_s * 1024:(par_s + 1) * 1024]
                lo = (blo - qlo) * 128
                hi = (qhi - qlo + 1) * 128

                def mk_st(e, j=j, blo=blo, qhi=qhi, pst=pst, lo=lo, hi=hi, kb=kb, q2b=q2b):
                    last = None
                    for h in range(2):
                        last = e.matmul(
                            pst[:, h * 512 + lo:h * 512 + hi],
                            kb[h * 64:(h + 1) * 64, j * 128:(j + 1) * 128],
                            q2b[h * 64:(h + 1) * 64, blo * 128:(qhi + 1) * 128],
                            start=True, stop=True, skip_group_check=True)
                    return last
                w = []
                if s_idx >= 2:
                    w.append((se, slot_exp_se[s_idx - 2]))
                else:
                    w += [(se, se_q2), (dv, dv_nm)]
                P.step("tensor", mk_st, waits=w, incs=[(pe, 1)])
                slot_mm_pe.append(pe.v)

                if j == 1 and p2g_pend[0] is not None:
                    p2g_tail(p2g_pend[0])
                    p2g_pend[0] = None

                if pend_av is not None:
                    aw = list(pend_av[1])
                    if pend_av[2] is not None:
                        aw.append((dv, r4dv[pend_av[2]]))
                    P.step("tensor", pend_av[0], waits=aw, incs=[(pe, 1)])
                    slot_av_pe.append(pe.v)
                    pend_av = None
                    if s_idx % 2 == 1:
                        drain_prep(1)

                ew = [(pe, slot_mm_pe[s_idx]), (dv, dv_nm)]
                if s_idx >= 2:
                    ew.append((pe, slot_av_pe[s_idx - 2]))
                P.step("scalar", lambda e, pst=pst, par_s=par_s, nmb=nmb: e.activation(
                    out=PT[par_s][:], in_=pst, func=AF.Exp, bias=nmb[:]),
                    waits=ew, incs=[(se, 1)])
                slot_exp_se.append(se.v)

                if qlo <= j // 4 <= qhi:
                    b = j // 4
                    srel = j - 4 * b
                    off = (b - qlo) * 128

                    def mk_msk(e, b=b, srel=srel, off=off, par_s=par_s):
                        last = None
                        for h in range(2):
                            sl = PT[par_s][:, h * 512 + off:h * 512 + off + 128]
                            last = e.tensor_tensor(
                                out=sl, in0=sl,
                                in1=maskT[:, (b * 4 + srel) * 128:(b * 4 + srel + 1) * 128],
                                op=ALU.mult)
                        return last
                    P.step("vector", mk_msk, waits=[(se, slot_exp_se[s_idx])], incs=[(dv, 1)])
                    diag_dv.append(dv.v)
                else:
                    diag_dv.append(None)

                def mk_av(e, j=j, jmax=jmax, par_s=par_s, lo=lo, hi=hi, vtb=vtb):
                    last = None
                    for h in range(2):
                        last = e.matmul(
                            psB[0:65, h * 512 + lo:h * 512 + hi],
                            vtb[:, (j * 2 + h) * 65:(j * 2 + h) * 65 + 65],
                            PT[par_s][:, h * 512 + lo:h * 512 + hi],
                            start=(j == 0), stop=(j == jmax - 1), skip_group_check=True)
                    return last
                avw = [(se, slot_exp_se[s_idx])]
                reg_free = None
                if diag_dv[s_idx] is not None:
                    avw.append((dv, diag_dv[s_idx]))
                if j == 0:
                    if q == 0:
                        avw.append((dv, dv_nm))
                    else:
                        reg_free = (qg - 1, 1)
                    avw.append((dv, dv_vtm))
                pend_av = (mk_av, avw, reg_free)
                s_idx += 1
            P.step("tensor", pend_av[0], waits=pend_av[1], incs=[(pe, 1)])
            slot_av_pe.append(pe.v)
            pend_av = None
            p2g_pend[0] = p2g_head(q, pe.v)
            QGI[0] += 1
        p2g_tail(p2g_pend[0])
        p2g_pend[0] = None
        prev_pair.update({
            "exp_se": slot_exp_se[-1],
            "av_pe": slot_av_pe[-1],
            "ot_dv": r4dv[(QGI[0] - 1, 1)],
        })

    emit_pair_weights(0, [])
    emit_pair_weights(1, [])
    fill_prep(0, ln_gate=True)

    for T in range(8):
        for k in range(8):
            i = 8 * T + k
            ln_jobs.append((din["xk"][i * 128:(i + 1) * 128, :],
                            xnf[i * 128:(i + 1) * 128, :]))
        for k in range(2):
            i = 2 * T + k
            ln_jobs.append((din["xo"][i * 128:(i + 1) * 128, :],
                            xnod[i * 128:(i + 1) * 128, :]))
    NJ = len(ln_jobs)
    chunks_done = [0]

    def maybe_chunk(i_store):
        while chunks_done[0] < 8:
            T = chunks_done[0]
            trigger = 10 * (T + 1) + 7 if T < 7 else NJ - 1
            if i_store != trigger:
                break
            emit_conv(2)
            drain_prep(4)
            chunks_done[0] += 1
            if T < 7:
                break
    for i in range(NJ + 2):
        if i < NJ:
            ln_stage_load_A(i)
        if 1 <= i <= NJ:
            ln_stage_BCD(i - 1)
        if i >= 2:
            ln_stage_E(i - 2)
            maybe_chunk(i - 2)
    drain_prep(len(prep))
    for p in range(NPAIR):
        if p + 1 < NPAIR:
            if p >= 1:
                emit_pair_weights(p + 1, [(pe, pstate[p - 1]["last_qpe"])])
            fill_prep(p + 1)
        emit_p2f(p)
        drain_prep(len(prep))

    emit_conv(len(conv_jobs))
    qg_conv_all = qG.v
    dv_ot_final = prev_pair["ot_dv"]
    if stop_after == "P2":
        return _early_exit("P2")

    p2s.close()
    wob = sb(pOTs, "wob", [128, 8 * 1024], BF16)
    y1_sb = [sb(pOTs, f"y1sb{i}", [128, D], F32) for i in range(2)]
    for pch in range(8):
        P.step("gpsimd", lambda e, pch=pch: e.dma_start(
            out=wob[:, pch * 1024:(pch + 1) * 1024],
            in_=din["wo"][pch * 128:(pch + 1) * 128, :]),
            waits=[(qG, qg_conv_all), (dv, dv_ot_final)] if pch == 0 else [],
            incs=[(qG, 16)])
    qg_wob = qG.v

    h_setf = []
    h_dvstats = []
    h_dvy1 = []
    h_qaxn2 = []
    h_sexn2 = []
    h_dvy1fin = []
    h_qgy1 = []
    for i in range(16):
        par = i % 2
        reg = psA[:, par * 1024:(par + 1) * 1024]
        lw = [(dv, h_dvy1[i - 2])] if i >= 2 else [(dv, dv_ot_final)]
        P.step("sync", lambda e, i=i, par=par: e.dma_start(
            out=xt[par][:], in_=din["xo"][i * 128:(i + 1) * 128, :]),
            waits=lw, incs=[(qS, 16)])
        qs_xo = qS.v

        def mk_wo(e, i=i, reg=reg):
            last = None
            for pch in range(8):
                st, sp = pch == 0, pch == 7
                lhsT = otbuf[:, pch * TOWN + i * 128:pch * TOWN + (i + 1) * 128]
                e.matmul(reg[:, 0:512], lhsT, wob[:, pch * 1024:pch * 1024 + 512],
                         start=st, stop=sp, skip_group_check=True)
                last = e.matmul(reg[:, 512:1024], lhsT, wob[:, pch * 1024 + 512:(pch + 1) * 1024],
                                start=st, stop=sp, skip_group_check=True)
            return last
        w = [(qG, qg_wob), (dv, dv_ot_final)]
        if i >= 2:
            w += [(se, h_setf[i - 2]), (dv, h_dvstats[i - 2])]
        P.step("tensor", mk_wo, waits=w, incs=[(pe, 1)])
        pe_wo = pe.v

        c = tuple(st_sm[:, k + 8 * par:k + 8 * par + 1] for k in range(4))
        dv_fin, dv_stats = emit_ln(reg, sqt[par][:], c,
                                   sq_waits=[(pe, pe_wo)], stats_waits=[(pe, pe_wo)])
        tw = [(dv, dv_fin)]
        if i >= 1:
            tw.append((dv, h_dvy1fin[i - 1]))
        P.step("scalar", lambda e, reg=reg, c=c: e.activation(
            out=t_sb[:], in_=reg, func=AF.Identity, bias=c[3], scale=c[2]),
            waits=tw, incs=[(se, 1)])
        h_setf.append(se.v)
        h_dvstats.append(dv_stats)
        yw = [(se, se.v), (qS, qs_xo)]
        if i >= 2:
            yw.append((qG, h_qgy1[i - 2]))
        P.step("vector", lambda e, par=par: e.tensor_add(
            out=y1_sb[par][:], in0=t_sb[:], in1=xt[par][:]),
            waits=yw, incs=[(dv, 1)])
        h_dvy1.append(dv.v)
        h_dvy1fin.append(dv.v)
        P.step("gpsimd", lambda e, i=i, par=par: e.dma_start(
            out=y1d[i * 128:(i + 1) * 128, :], in_=y1_sb[par][:]),
            waits=[(dv, dv.v)], incs=[(qG, 16)])
        h_qgy1.append(qG.v)

        dv_fin2, _ = emit_ln(y1_sb[par][:], sqt[par][:], c,
                             sq_waits=[(dv, h_dvy1[i])], stats_waits=[])
        nw = [(dv, dv_fin2)]
        if i >= 2:
            nw.append((qA, h_qaxn2[i - 2]))
        P.step("scalar", lambda e, par=par, c=c: e.activation(
            out=xnt[par][:], in_=y1_sb[par][:], func=AF.Identity, bias=c[3], scale=c[2]),
            waits=nw, incs=[(se, 1)])
        h_sexn2.append(se.v)
        P.step("scalar", lambda e, i=i, par=par: e.dma_start(
            out=xn2d[i * 128:(i + 1) * 128, :], in_=xnt[par][:]),
            waits=[(se, h_sexn2[-1])], incs=[(qA, 16)])
        h_qaxn2.append(qA.v)
    qa_p3a = qA.v
    qg_p3a = qG.v
    if stop_after == "P3a":
        return _early_exit("P3a")
    se_p3a = se.v
    dv_p3a = dv.v

    if debug:
        P.step("gpsimd", lambda e: e.dma_start(out=dbg["y1"], in_=y1d),
               waits=[(qG, qg_p3a), (qA, qa_p3a)], incs=[(qG, 16)])
        P.step("gpsimd", lambda e: e.dma_start(out=dbg["xn2"], in_=xn2d),
               waits=[(qG, qg_p3a), (qA, qa_p3a)], incs=[(qG, 16)])

    pOTs.close()
    out_sb = [sb(es, f"outsb{i}", [128, D], F32) for i in range(2)]
    A_buf = sb(es, "A_buf", [128, 32 * 512], BF16)
    B_buf = sb(es, "B_buf", [128, 32 * 512], BF16)
    xn2f = sb(es, "xn2f", [128, 8 * 512], BF16)
    w12s = [sb(es, f"w12s{i}", [128, 2 * 8 * 128], BF16) for i in range(4)]
    hstr = [sb(es, f"hstr{i}", [128, 32 * 128], BF16) for i in range(2)]
    w3s = [sb(es, f"w3s{i}", [128, 1024], BF16) for i in range(4)]
    hA = [sb(es, f"hA{i}", [128, 512], F32) for i in range(2)]
    hG = [sb(es, f"hG{i}", [128, 512], F32) for i in range(2)]
    f_sb = [sb(es, f"fsb{i}", [128, D], F32) for i in range(2)]

    n_out = 0
    prev_f_dv = dv_p3a
    prev_w1_pe = pe.v
    h_dvoutfin = []
    h_qout = []
    h_se_t3 = []
    gidx = 0
    for T in range(4):
        t0 = T * 512
        for j in range(8):
            w = []
            if T == 0 and j == 0:
                w.append((qA, qa_p3a))
            if T > 0 and j == 0:
                w.append((pe, prev_w1_pe))
            P.step("sync", lambda e, j=j, t0=t0: e.dma_start(
                out=xn2f[:, j * 512:(j + 1) * 512],
                in_=xn2d[t0:t0 + 512, j * 128:(j + 1) * 128], transpose=True),
                waits=w, incs=[(qS, 16)])
        qs_xn2f = qS.v

        w12_pe = {}
        w12_se = {}
        w12_dv = {}
        for m in range(32):
            par = m % 4
            par2 = m % 2
            lw = [(pe, w12_pe[m - 4])] if m >= 4 else []
            P.step("sync", lambda e, m=m, par=par: e.dma_start(
                out=w12s[par][:, 0:1024], in_=wc1[m]), waits=lw, incs=[(qS, 16)])
            P.step("sync", lambda e, m=m, par=par: e.dma_start(
                out=w12s[par][:, 1024:2048], in_=wc2[m]), incs=[(qS, 16)])
            qs_w = qS.v
            bank = psA if par < 2 else psB
            ps1 = bank[:, (par % 2) * 512:(par % 2) * 512 + 512]
            ps2 = bank[:, 1024 + (par % 2) * 512:1024 + (par % 2) * 512 + 512]

            def mk12(e, par=par, ps1=ps1, ps2=ps2):
                last = None
                for j in range(8):
                    st, sp = j == 0, j == 7
                    rhs = xn2f[:, j * 512:(j + 1) * 512]
                    e.matmul(ps1, w12s[par][:, j * 128:(j + 1) * 128], rhs,
                             start=st, stop=sp, skip_group_check=True)
                    last = e.matmul(ps2, w12s[par][:, 1024 + j * 128:1024 + (j + 1) * 128], rhs,
                                    start=st, stop=sp, skip_group_check=True)
                return last
            w = [(qS, qs_w)]
            if m >= 4:
                w += [(se, w12_se[m - 4]), (dv, w12_dv[m - 4])]
            else:
                w.append((dv, prev_f_dv))
                if T == 0:
                    w += [(se, se_p3a), (dv, dv_p3a), (qG, qg_p3a)]
            P.step("tensor", mk12, waits=w, incs=[(pe, 1)])
            w12_pe[m] = pe.v
            P.step("scalar", lambda e, ps1=ps1, m=m, par2=par2: e.activation(
                out=hA[par2][:], in_=ps1, func=AF.Silu, scale=scl_s1[:, m:m + 1]),
                waits=[(pe, pe.v)] + ([(dv, w12_dv[m - 2])] if m >= 2 else []),
                incs=[(se, 1)])
            w12_se[m] = se.v
            P.step("vector", lambda e, ps2=ps2, m=m, par2=par2: e.tensor_scalar(
                out=hG[par2][:], in0=ps2, scalar1=scl_s2[:, m:m + 1], scalar2=None, op0=ALU.mult),
                waits=[(pe, w12_pe[m])], incs=[(dv, 1)])
            P.step("vector", lambda e, m=m, par2=par2: e.tensor_tensor(
                out=A_buf[:, m * 512:(m + 1) * 512], in0=hA[par2][:], in1=hG[par2][:],
                op=ALU.mult),
                waits=[(se, w12_se[m])], incs=[(dv, 1)])
            w12_dv[m] = dv.v
        dv_A = dv.v
        prev_w1_pe = w12_pe[31]
        if debug and T == 0:
            P.step("gpsimd", lambda e: e.dma_start(out=dbg["A0"], in_=A_buf[:]),
                   waits=[(dv, dv_A)], incs=[(qG, 16)])
            dv_A = dv.v
            h1_dbg_qg = qG.v

        h1_pe = {}
        h1_se = {}
        for m in range(32):
            par = m % 4
            lw = [(pe, h1_pe[m - 2])] if m >= 2 else [(pe, w12_pe[30 + m])]
            P.step("sync", lambda e, m=m, par=par % 2: e.dma_start(
                out=hstr[par][:], in_=wch1[m]), waits=lw, incs=[(qS, 16)])
            qs_h = qS.v
            bank = psA if par < 2 else psB
            ps1 = bank[:, (par % 2) * 512:(par % 2) * 512 + 512]

            def mkh1(e, par=par % 2, ps1=ps1):
                last = None
                for j in range(32):
                    last = e.matmul(ps1, hstr[par][:, j * 128:(j + 1) * 128],
                                    A_buf[:, j * 512:(j + 1) * 512],
                                    start=(j == 0), stop=(j == 31), skip_group_check=True)
                return last
            w = [(qS, qs_h), (dv, dv_A)]
            if m >= 4:
                w.append((se, h1_se[m - 4]))
            else:
                w.append((se, w12_se[28 + m]))
            P.step("tensor", mkh1, waits=w, incs=[(pe, 1)])
            h1_pe[m] = pe.v
            P.step("scalar", lambda e, ps1=ps1, m=m: e.activation(
                out=B_buf[:, m * 512:(m + 1) * 512], in_=ps1, func=AF.Gelu,
                scale=scl_h1[:, m:m + 1]),
                waits=[(pe, pe.v)], incs=[(se, 1)])
            h1_se[m] = se.v
        se_B = se.v
        if debug and T == 0:
            P.step("gpsimd", lambda e: e.dma_start(out=dbg["B0"], in_=B_buf[:]),
                   waits=[(se, se_B)], incs=[(qG, 16)])

        h2_pe = {}
        h2_se = {}
        for m in range(32):
            par = m % 4
            lw = [(pe, h2_pe[m - 2])] if m >= 2 else [(pe, h1_pe[30 + m])]
            P.step("sync", lambda e, m=m, par=par % 2: e.dma_start(
                out=hstr[par][:], in_=wch2[m]), waits=lw, incs=[(qS, 16)])
            qs_h = qS.v
            bank = psA if par < 2 else psB
            ps1 = bank[:, (par % 2) * 512:(par % 2) * 512 + 512]

            def mkh2(e, par=par % 2, ps1=ps1):
                last = None
                for j in range(32):
                    last = e.matmul(ps1, hstr[par][:, j * 128:(j + 1) * 128],
                                    B_buf[:, j * 512:(j + 1) * 512],
                                    start=(j == 0), stop=(j == 31), skip_group_check=True)
                return last
            w = [(qS, qs_h), (se, se_B)]
            if m >= 4:
                w.append((se, h2_se[m - 4]))
            else:
                w.append((se, h1_se[28 + m]))
            P.step("tensor", mkh2, waits=w, incs=[(pe, 1)])
            h2_pe[m] = pe.v
            h2w = [(pe, pe.v)]
            if debug and T == 0 and m < 2:
                h2w.append((qG, h1_dbg_qg))
            P.step("scalar", lambda e, ps1=ps1, m=m: e.activation(
                out=A_buf[:, m * 512:(m + 1) * 512], in_=ps1, func=AF.Gelu,
                scale=scl_h2[:, m:m + 1]),
                waits=h2w, incs=[(se, 1)])
            h2_se[m] = se.v
        se_A2 = se.v
        if debug and T == 0:
            P.step("gpsimd", lambda e: e.dma_start(out=dbg["A2"], in_=A_buf[:]),
                   waits=[(se, se_A2)], incs=[(qG, 16)])

        w3_pe = {}
        for j in range(32):
            par = j % 4
            lw = [(pe, w3_pe[j - 4])] if j >= 4 else []
            P.step("sync", lambda e, j=j, par=par: e.dma_start(
                out=w3s[par][:], in_=wc3[j * 128:(j + 1) * 128, :]),
                waits=lw, incs=[(qS, 16)])
            qs_w3 = qS.v

            def mkw3(e, j=j, par=par):
                last = None
                for sI in range(4):
                    ps = psA if sI < 2 else psB
                    regw = ps[:, (sI % 2) * 1024:(sI % 2) * 1024 + 1024]
                    lhsT = A_buf[:, j * 512 + sI * 128:j * 512 + (sI + 1) * 128]
                    e.matmul(regw[:, 0:512], lhsT, w3s[par][:, 0:512],
                             start=(j == 0), stop=(j == 31), skip_group_check=True)
                    last = e.matmul(regw[:, 512:1024], lhsT, w3s[par][:, 512:1024],
                                    start=(j == 0), stop=(j == 31), skip_group_check=True)
                return last
            w = [(qS, qs_w3), (se, se_A2)]
            if j == 0:
                w.append((dv, prev_f_dv))
                if T == 0:
                    w.append((dv, dv_p3a))
            P.step("tensor", mkw3, waits=w, incs=[(pe, 1)])
            w3_pe[j] = pe.v
        pe_w3 = pe.v

        for sI in range(4):
            par = sI % 2
            ps = psA if sI < 2 else psB
            regw = ps[:, (sI % 2) * 1024:(sI % 2) * 1024 + 1024]
            fw = [(pe, pe_w3), (dv, dv_s3b)]
            if gidx >= 2:
                fw.append((se, h_se_t3[gidx - 2]))
                if debug and T == 0:
                    fw.append((qG, f_dbg_qg))
            P.step("vector", lambda e, regw=regw, par=par: e.tensor_tensor(
                out=f_sb[par][:], in0=regw, in1=s3B[:], op=ALU.mult),
                waits=fw, incs=[(dv, 1)])
            P.step("vector", lambda e: e.tensor_copy(out=st_sm[0:1, 14:15], in_=st_sm[0:1, 14:15]),
                   incs=[(dv, 1)])
            dv_f = dv.v
            if sI == 3:
                psum_free_dv = dv_f
            if debug and T == 0:
                P.step("gpsimd", lambda e, sI=sI, par=par: e.dma_start(
                    out=dbg["F0"][:, sI * 1024:(sI + 1) * 1024], in_=f_sb[par][:]),
                    waits=[(dv, dv_f)], incs=[(qG, 16)])
                f_dbg_qg = qG.v
            lw = [(dv, h_dvoutfin[gidx - 2])] if gidx >= 2 else [(dv, h_dvy1[14 + gidx])]
            P.step("sync", lambda e, t0=t0, sI=sI, par=par: e.dma_start(
                out=xt[par][:], in_=y1d[t0 + sI * 128:t0 + (sI + 1) * 128, :]),
                waits=lw, incs=[(qS, 16)])
            qs_y1 = qS.v
            c = tuple(st_sm[:, k + 8 * par:k + 8 * par + 1] for k in range(4))
            dv_fin3, _ = emit_ln(f_sb[par][:], sqt[par][:], c,
                                 sq_waits=[(dv, dv_f)], stats_waits=[])
            tw = [(dv, dv_fin3)]
            if gidx >= 1:
                tw.append((dv, h_dvoutfin[gidx - 1]))
            P.step("scalar", lambda e, par=par, c=c: e.activation(
                out=t_sb[:], in_=f_sb[par][:], func=AF.Identity, bias=c[3], scale=c[2]),
                waits=tw, incs=[(se, 1)])
            P.step("scalar", lambda e: e.activation(out=st_sm[0:1, 15:16],
                                                    in_=st_sm[0:1, 15:16], func=AF.Copy),
                   incs=[(se, 1)])
            h_se_t3.append(se.v)
            if debug and T == 0:
                P.step("gpsimd", lambda e, sI=sI: e.dma_start(
                    out=dbg["T0"][:, sI * 1024:(sI + 1) * 1024], in_=t_sb[:]),
                    waits=[(se, se.v)], incs=[(qG, 16)])
                P.step("gpsimd", lambda e, sI=sI, par=par: e.dma_start(
                    out=dbg["X0"][:, sI * 1024:(sI + 1) * 1024], in_=xt[par][:]),
                    waits=[(qS, qs_y1)], incs=[(qG, 16)])
                t_dbg_qg = qG.v
            ow = [(se, se.v), (qS, qs_y1)]
            if debug and T == 0:
                ow.append((qG, t_dbg_qg))
            if gidx >= 2:
                ow.append((st_out, 16 * (gidx - 1)))
            P.step("vector", lambda e, par=par: e.tensor_add(
                out=out_sb[par][:], in0=t_sb[:], in1=xt[par][:]),
                waits=ow, incs=[(dv, 1)])
            P.step("vector", lambda e: e.tensor_copy(out=st_sm[0:1, 14:15], in_=st_sm[0:1, 14:15]),
                   incs=[(dv, 1)])
            h_dvoutfin.append(dv.v)
            P.step("gpsimd", lambda e, t0=t0, sI=sI, par=par: e.dma_start(
                out=out[t0 + sI * 128:t0 + (sI + 1) * 128, :], in_=out_sb[par][:]),
                waits=[(dv, dv.v)], incs=[(st_out, 16)])
            h_qout.append(st_out.v)
            n_out += 1
            gidx += 1
        prev_f_dv = psum_free_dv

    P.step("gpsimd", lambda e: e.engine_nop(), waits=[(st_out, 16 * n_out)])

    with es:
        block = es.enter_context(nc.Block())
        P.emit(block)
    return nc


def _tiles_for_core(c):
    cp = c % 4
    return [4 * b + ((cp + b) % 4) for b in range(NL)]


def _stage(inputs):
    x = np.ascontiguousarray(np.asarray(inputs["L0_attn_x"], dtype=np.float32))
    w = {k: np.ascontiguousarray(np.asarray(v, dtype=np.float32)) for k, v in inputs.items()
         if k != "L0_attn_x"}
    shared = dict(
        wq=w["L0_attn_WQ"], wk=w["L0_attn_WK"], wv=w["L0_attn_WV"], wo=w["L0_attn_WO"],
        w1t=np.ascontiguousarray(w["L0_ffn_w1_w"].T),
        w2t=np.ascontiguousarray(w["L0_ffn_w2_w"].T),
        h1t=np.ascontiguousarray(w["L0_ffn_h1_w"].T),
        h2t=np.ascontiguousarray(w["L0_ffn_h2_w"].T),
        w3t=np.ascontiguousarray(w["L0_ffn_w3_w"].T),
        s1c=np.ascontiguousarray(w["L0_ffn_w1_s"].reshape(32, 128).T),
        s2c=np.ascontiguousarray(w["L0_ffn_w2_s"].reshape(32, 128).T),
        hs1c=np.ascontiguousarray(w["L0_ffn_h1_s"].reshape(32, 128).T),
        hs2c=np.ascontiguousarray(w["L0_ffn_h2_s"].reshape(32, 128).T),
        s3r=np.ascontiguousarray(w["L0_ffn_w3_s"].reshape(1, D)),
        identb=np.eye(128, dtype=ml_dtypes.bfloat16),
        ones_c1=np.ones((128, 1), dtype=ml_dtypes.bfloat16),
        ones_r128=np.ones((1, 128), dtype=ml_dtypes.bfloat16),
        ones_r64f=np.ones((1, 64), dtype=np.float32),
        ones_r128f=np.ones((1, 128), dtype=np.float32),
        epsb=np.full((128, 1), EPS, dtype=np.float32),
    )
    maps = []
    pidx = np.arange(128)[:, None]
    qidx = np.arange(128)[None, :]
    for c in range(8):
        g = c // 4
        tiles = _tiles_for_core(c)
        xo = np.concatenate([x[g, t * 128:(t + 1) * 128, :] for t in tiles], axis=0)
        maskT = np.zeros((128, NL * 4 * 128), dtype=np.float32)
        for b in range(NL):
            tb = tiles[b]
            for sblk in range(4):
                kglob = 512 * b + 128 * sblk + pidx
                qglob = tb * 128 + qidx
                maskT[:, (b * 4 + sblk) * 128:(b * 4 + sblk + 1) * 128] = (kglob <= qglob)
        m = dict(shared)
        m["xk"] = np.ascontiguousarray(x[g])
        m["xo"] = np.ascontiguousarray(xo)
        m["maskT"] = maskT.astype(ml_dtypes.bfloat16)
        maps.append(m)
    return maps


def kernel(**inputs):
    maps = _stage(inputs)
    nc = build_nc(debug=False)
    res = run_bass_kernel_spmd(nc, maps, core_ids=list(range(8)))
    y = np.empty((BATCH, S, D), dtype=np.float32)
    for c in range(8):
        g = c // 4
        o = np.asarray(res.results[c]["out"], dtype=np.float32)
        for b, t in enumerate(_tiles_for_core(c)):
            y[g, t * 128:(t + 1) * 128, :] = o[b * 128:(b + 1) * 128, :]
    return y


# version 10
# speedup vs baseline: 1.0711x; 1.0711x over previous
import sys

sys.path.insert(0, "/opt/trn_rl_repo")
import contextlib

import numpy as np
import ml_dtypes

import concourse.bass as bass
import concourse.mybir as mybir
from concourse.bass_utils import run_bass_kernel_spmd

F32 = mybir.dt.float32
BF16 = mybir.dt.bfloat16
AF = mybir.ActivationFunctionType
ALU = mybir.AluOpType
AX = mybir.AxisListType

D = 1024
S = 8192
BATCH = 2
NL = 16
TOWN = NL * 128
NPAIR = 8
EPS = 1e-5


class Sem:
    def __init__(self, handle):
        self.h = handle
        self.v = 0


class Prog:
    def __init__(self):
        self.ops = []

    def step(self, eng, builder, waits=(), incs=()):
        waits = tuple((s, v) for s, v in waits)
        incs = tuple(incs)
        for s, n in incs:
            s.v += n

        def fn(e, builder=builder, waits=waits, incs=incs):
            for sem, val in waits:
                if val > 0:
                    e.wait_ge(sem.h, val)
            inst = builder(e)
            for sem, n in incs:
                inst.then_inc(sem.h, n)

        self.ops.append((eng, fn))

    def emit(self, block):
        for name in ("sync", "gpsimd", "vector", "scalar", "tensor"):
            ops = [fn for eng, fn in self.ops if eng == name]
            if not ops:
                continue

            def make(ops):
                def run(e):
                    for fn in ops:
                        fn(e)
                return run

            getattr(block, name)(make(ops))


def build_nc(debug=False, stop_after=None):
    nc = bass.Bass("TRN2", target_bir_lowering=False)

    din = {}
    for name, shape, dt in [
        ("xk", [S, D], F32), ("xo", [TOWN, D], F32),
        ("wq", [D, D], F32), ("wk", [D, D], F32), ("wv", [D, D], F32), ("wo", [D, D], F32),
        ("w1t", [D, 4096], F32), ("w2t", [D, 4096], F32),
        ("h1t", [4096, 4096], F32), ("h2t", [4096, 4096], F32),
        ("w3t", [4096, D], F32),
        ("s1c", [128, 32], F32), ("s2c", [128, 32], F32),
        ("hs1c", [128, 32], F32), ("hs2c", [128, 32], F32),
        ("s3r", [1, D], F32),
        ("maskT", [128, NL * 4 * 128], BF16),
        ("identb", [128, 128], BF16),
        ("ones_c1", [128, 1], BF16),
        ("ones_r128", [1, 128], BF16),
        ("ones_r64f", [1, 64], F32),
        ("ones_r128f", [1, 128], F32),
        ("epsb", [128, 1], F32),
    ]:
        din[name] = nc.dram_tensor(name, shape, dt, kind="ExternalInput").ap()

    out = nc.dram_tensor("out", [TOWN, D], F32, kind="ExternalOutput").ap()
    dbg = {}
    if debug:
        dbg["xnf"] = nc.dram_tensor("dbg_xnf", [S, D], BF16, kind="ExternalOutput").ap()
        dbg["ot"] = nc.dram_tensor("dbg_ot", [128, NPAIR * TOWN], BF16, kind="ExternalOutput").ap()
        dbg["y1"] = nc.dram_tensor("dbg_y1", [TOWN, D], F32, kind="ExternalOutput").ap()
        dbg["A0"] = nc.dram_tensor("dbg_A0", [128, 32 * 512], BF16, kind="ExternalOutput").ap()
        dbg["B0"] = nc.dram_tensor("dbg_B0", [128, 32 * 512], BF16, kind="ExternalOutput").ap()
        dbg["A2"] = nc.dram_tensor("dbg_A2", [128, 32 * 512], BF16, kind="ExternalOutput").ap()
        dbg["F0"] = nc.dram_tensor("dbg_F0", [128, 4 * 1024], F32, kind="ExternalOutput").ap()
        dbg["T0"] = nc.dram_tensor("dbg_T0", [128, 4 * 1024], F32, kind="ExternalOutput").ap()
        dbg["X0"] = nc.dram_tensor("dbg_X0", [128, 4 * 1024], F32, kind="ExternalOutput").ap()
        dbg["xn2"] = nc.dram_tensor("dbg_xn2", [TOWN, D], BF16, kind="ExternalOutput").ap()

    xnf = nc.dram_tensor("xnf", [S, D], BF16, kind="Internal").ap()
    xnod = nc.dram_tensor("xnod", [TOWN, D], BF16, kind="Internal").ap()
    y1d = nc.dram_tensor("y1d", [TOWN, D], F32, kind="Internal").ap()
    xn2d = nc.dram_tensor("xn2d", [TOWN, D], BF16, kind="Internal").ap()
    wc1 = nc.dram_tensor("wc1", [32, 128, 8 * 128], BF16, kind="Internal").ap()
    wc2 = nc.dram_tensor("wc2", [32, 128, 8 * 128], BF16, kind="Internal").ap()
    wch1 = nc.dram_tensor("wch1", [32, 128, 32 * 128], BF16, kind="Internal").ap()
    wch2 = nc.dram_tensor("wch2", [32, 128, 32 * 128], BF16, kind="Internal").ap()
    wc3 = nc.dram_tensor("wc3", [4096, D], BF16, kind="Internal").ap()

    es = contextlib.ExitStack()
    pOTs = contextlib.ExitStack()
    p2s = contextlib.ExitStack()

    def sb(stack, name, shape, dt):
        return stack.enter_context(nc.sbuf_tensor(name, shape, dt))

    s3B = sb(es, "s3B", [128, D], F32)
    ident = sb(es, "ident", [128, 128], BF16)
    ones_c1 = sb(es, "ones_c1s", [128, 1], BF16)
    ones_r128 = sb(es, "ones_r128s", [1, 128], BF16)
    ones_r64f = sb(es, "ones_r64fs", [1, 64], F32)
    ones_r128f = sb(es, "ones_r128fs", [1, 128], F32)
    epsB = sb(es, "epsB", [128, 1], F32)
    scl_s1 = sb(es, "scl_s1", [128, 32], F32)
    scl_s2 = sb(es, "scl_s2", [128, 32], F32)
    scl_h1 = sb(es, "scl_h1", [128, 32], F32)
    scl_h2 = sb(es, "scl_h2", [128, 32], F32)
    xt = [sb(es, f"xt{i}", [128, D], F32) for i in range(4)]
    sqt = [sb(es, f"sqt{i}", [128, D], F32) for i in range(2)]
    s3r_sb = sqt[1][0:1, 0:D]
    xnt = [sb(es, f"xnt{i}", [128, D], BF16) for i in range(4)]
    st_sm = sb(es, "st_sm", [128, 16], F32)
    t_sb = sb(es, "t_sb", [128, D], F32)

    otbuf = sb(pOTs, "otbuf", [128, NPAIR * TOWN], BF16)

    K2 = [sb(p2s, f"K2{i}", [128, S], BF16) for i in range(2)]
    V2 = sb(p2s, "V2", [128, S], BF16)
    Vtm = [sb(p2s, f"Vtm{i}", [128, 64 * 2 * 65], BF16) for i in range(2)]
    Q2 = [sb(p2s, f"Q2_{i}", [128, TOWN], BF16) for i in range(2)]
    maskT = sb(p2s, "maskT_s", [128, NL * 4 * 128], BF16)
    xnstr = [sb(p2s, f"xnstr{i}", [128, 2048], BF16) for i in range(2)]
    wkb = [sb(p2s, f"wkb{i}", [128, 8 * 128], BF16) for i in range(2)]
    wvb = [sb(p2s, f"wvb{i}", [128, 8 * 128], BF16) for i in range(2)]
    wqb = [sb(p2s, f"wqb{i}", [128, 8 * 128], BF16) for i in range(2)]
    PT = [sb(p2s, f"PT{i}", [128, 1024], BF16) for i in range(2)]
    nmB = [sb(p2s, f"nmB{i}", [128, 1], F32) for i in range(2)]
    nmb16 = sb(p2s, "nmb16", [1, 1], BF16)
    sm1 = sb(p2s, "sm1", [1, 8], F32)
    rinv = sqt[0][0:1, 0:1024]
    rB = sqt[1][0:64, 0:1024]
    conv = [sb(p2s, "conv0", [128, 16 * 128], BF16),
            otbuf[:, 7 * TOWN:7 * TOWN + 16 * 128]]

    psA = es.enter_context(nc.psum_tensor("psA", [128, 2048], F32))
    psB = es.enter_context(nc.psum_tensor("psB", [128, 2048], F32))

    qS = Sem(es.enter_context(nc.semaphore("qS")))
    qG = Sem(es.enter_context(nc.semaphore("qG")))
    qA = Sem(es.enter_context(nc.semaphore("qA")))
    pe = Sem(es.enter_context(nc.semaphore("pe")))
    se = Sem(es.enter_context(nc.semaphore("se")))
    dv = Sem(es.enter_context(nc.semaphore("dv")))
    st_out = Sem(es.enter_context(nc.semaphore("st_out")))

    P = Prog()

    for dst, src in [
        (s3r_sb[:], din["s3r"]), (ident[:], din["identb"]),
        (ones_c1[:], din["ones_c1"]), (ones_r128[:], din["ones_r128"]),
        (ones_r64f[:], din["ones_r64f"]), (ones_r128f[:], din["ones_r128f"]),
        (epsB[:], din["epsb"]),
        (scl_s1[:], din["s1c"]), (scl_s2[:], din["s2c"]),
        (scl_h1[:], din["hs1c"]), (scl_h2[:], din["hs2c"]),
        (maskT[:], din["maskT"]),
    ]:
        P.step("sync", lambda e, d=dst, s=src: e.dma_start(out=d, in_=s), incs=[(qS, 16)])
    qs_consts = qS.v

    for vtb in Vtm:
        vt_ones = vtb.ap().rearrange("p (b c) -> p b c", c=65)[:, :, 64:65]
        P.step("vector", lambda e, o=vt_ones: e.memset(o, 1.0), incs=[(dv, 1)])
    dv_vtones = dv.v

    def mk_s3b(e):
        e.matmul(psA[:, 0:512], ones_r128f[:], s3r_sb[0:1, 0:512],
                 start=True, stop=True, skip_group_check=True)
        return e.matmul(psA[:, 512:1024], ones_r128f[:], s3r_sb[0:1, 512:1024],
                        start=True, stop=True, skip_group_check=True)
    P.step("tensor", mk_s3b, waits=[(qS, qs_consts)], incs=[(pe, 1)])
    P.step("vector", lambda e: e.tensor_copy(out=s3B[:], in_=psA[:, 0:1024]),
           waits=[(pe, pe.v)], incs=[(dv, 1)])
    dv_s3b = dv.v

    def _early_exit(tag):
        P.step("gpsimd", lambda e: e.dma_start(out=out[0:128, :], in_=s3B[:]),
               waits=[(dv, dv.v), (se, se.v), (qS, qS.v), (qG, qG.v), (qA, qA.v)],
               incs=[(st_out, 16)])
        P.step("gpsimd", lambda e: e.engine_nop(), waits=[(st_out, st_out.v)])
        for stk in (p2s, pOTs):
            try:
                stk.close()
            except Exception:
                pass
        with es:
            block = es.enter_context(nc.Block())
            P.emit(block)
        return nc

    conv_jobs = []
    for m in range(32):
        conv_jobs.append((din["w1t"][:, m * 128:(m + 1) * 128]
                          .rearrange("(j p) d -> p j d", p=128), wc1[m], 8 * 128))
        conv_jobs.append((din["w2t"][:, m * 128:(m + 1) * 128]
                          .rearrange("(j p) d -> p j d", p=128), wc2[m], 8 * 128))
    for m in range(32):
        for half in range(2):
            conv_jobs.append((din["h1t"][half * 2048:(half + 1) * 2048, m * 128:(m + 1) * 128]
                              .rearrange("(j p) d -> p j d", p=128),
                              wch1[m][:, half * 2048:(half + 1) * 2048], 16 * 128))
            conv_jobs.append((din["h2t"][half * 2048:(half + 1) * 2048, m * 128:(m + 1) * 128]
                              .rearrange("(j p) d -> p j d", p=128),
                              wch2[m][:, half * 2048:(half + 1) * 2048], 16 * 128))
    for j in range(32):
        conv_jobs.append((din["w3t"][j * 128:(j + 1) * 128, :],
                          wc3[j * 128:(j + 1) * 128, :], D))
    cstate = {"i": 0, "done": []}

    def emit_conv(n):
        for _ in range(n):
            i = cstate["i"]
            if i >= len(conv_jobs):
                return
            src, dst, ncol = conv_jobs[i]
            buf = conv[i % 2][:, 0:ncol]
            if len(src.shape) == 3:
                buf = buf.rearrange("p (j d) -> p j d", d=128)
            w = [(qG, cstate["done"][i - 2])] if i >= 2 else []
            P.step("gpsimd", lambda e, b=buf, s=src: e.dma_start(out=b, in_=s),
                   waits=w, incs=[(qG, 16)])
            ldv = qG.v
            P.step("gpsimd", lambda e, b=buf, d=dst: e.dma_start(out=d, in_=b),
                   waits=[(qG, ldv)], incs=[(qG, 16)])
            cstate["done"].append(qG.v)
            cstate["i"] += 1

    def emit_ln(x_ap, sq_ap, c, sq_waits, stats_waits=()):
        P.step("scalar", lambda e: e.activation(out=sq_ap, in_=x_ap, func=AF.Square),
               waits=sq_waits, incs=[(se, 1)])
        se_sq = se.v

        def reds(e):
            e.reduce_sum(out=c[0], in_=x_ap, axis=AX.X)
            return e.reduce_sum(out=c[1], in_=sq_ap, axis=AX.X)
        P.step("vector", reds, waits=list(stats_waits) + [(se, se_sq)], incs=[(dv, 1)])
        dv_stats = dv.v
        P.step("vector", lambda e: e.scalar_tensor_tensor(
            out=c[2], in0=c[0], scalar=1.0 / (D * D), in1=c[0], op0=ALU.mult, op1=ALU.mult),
            waits=[(dv, dv_stats)], incs=[(dv, 1)])
        P.step("vector", lambda e: e.scalar_tensor_tensor(
            out=c[1], in0=c[1], scalar=1.0 / D, in1=c[2], op0=ALU.mult, op1=ALU.subtract),
            waits=[(dv, dv.v)], incs=[(dv, 1)])
        P.step("scalar", lambda e: e.activation(out=c[2], in_=c[1], func=AF.Sqrt, bias=epsB[:]),
               waits=[(dv, dv.v)], incs=[(se, 1)])
        P.step("vector", lambda e: e.reciprocal(out=c[2], in_=c[2]),
               waits=[(se, se.v)], incs=[(dv, 1)])
        P.step("vector", lambda e: e.scalar_tensor_tensor(
            out=c[3], in0=c[0], scalar=-1.0 / D, in1=c[2], op0=ALU.mult, op1=ALU.mult),
            waits=[(dv, dv.v)], incs=[(dv, 1)])
        return dv.v, dv_stats

    ln_jobs = []
    ln_hist = {}
    qa_hist = {}
    lnc = [tuple(st_sm[:, k + 4 * par:k + 4 * par + 1] for k in range(4))
           for par in range(4)]

    def ln_stage_load_A(i):
        src, dst = ln_jobs[i]
        par = i % 4
        lw = [(se, ln_hist[i - 4]["E"])] if i >= 4 else []
        P.step("sync", lambda e, d=xt[par][:], s=src: e.dma_start(out=d, in_=s),
               waits=lw, incs=[(qS, 16)])
        qsv = qS.v
        c = lnc[par]
        aw = [(qS, qsv)]
        if i >= 4:
            aw.append((qA, qa_hist[i - 4]))
        P.step("scalar", lambda e, par=par, c=c: e.activation(
            out=xnt[par][:], in_=xt[par][:], func=AF.Square, accum_out=c[1]),
            waits=aw, incs=[(se, 1)])
        ln_hist[i] = {"A": se.v}

    def ln_stage_BCD(i):
        par = i % 4
        c = lnc[par]
        h = ln_hist[i]
        P.step("vector", lambda e, par=par, c=c: e.reduce_sum(
            out=c[0], in_=xt[par][:], axis=AX.X), waits=[(se, h["A"])], incs=[(dv, 1)])
        P.step("vector", lambda e, c=c: e.scalar_tensor_tensor(
            out=c[2], in0=c[0], scalar=1.0 / (D * D), in1=c[0], op0=ALU.mult, op1=ALU.mult),
            waits=[(dv, dv.v)], incs=[(dv, 1)])
        P.step("vector", lambda e, c=c: e.scalar_tensor_tensor(
            out=c[1], in0=c[1], scalar=1.0 / D, in1=c[2], op0=ALU.mult, op1=ALU.subtract),
            waits=[(dv, dv.v)], incs=[(dv, 1)])
        P.step("scalar", lambda e, c=c: e.activation(
            out=c[2], in_=c[1], func=AF.Sqrt, bias=epsB[:]),
            waits=[(dv, dv.v)], incs=[(se, 1)])
        P.step("vector", lambda e, c=c: e.reciprocal(out=c[2], in_=c[2]),
               waits=[(se, se.v)], incs=[(dv, 1)])
        P.step("vector", lambda e, c=c: e.scalar_tensor_tensor(
            out=c[3], in0=c[0], scalar=-1.0 / D, in1=c[2], op0=ALU.mult, op1=ALU.mult),
            waits=[(dv, dv.v)], incs=[(dv, 1)])
        h["D"] = dv.v

    def ln_stage_E(i):
        src, dst = ln_jobs[i]
        par = i % 4
        c = lnc[par]
        h = ln_hist[i]
        nw = [(dv, h["D"])]
        P.step("scalar", lambda e, par=par, c=c: e.activation(
            out=xnt[par][:], in_=xt[par][:], func=AF.Identity, bias=c[3], scale=c[2]),
            waits=nw, incs=[(se, 1)])
        h["E"] = se.v
        P.step("scalar", lambda e, par=par, d=dst: e.dma_start(out=d, in_=xnt[par][:]),
               waits=[(se, h["E"])], incs=[(qA, 16)])
        qa_hist[i] = qA.v

    pstate = [dict() for _ in range(NPAIR)]
    prev_pair = {"exp_se": 0, "av_pe": 0, "ot_dv": dv_s3b, "nm_pe": 0}
    r4dv = {}
    QGI = [0]
    prep = []

    def drain_prep(n):
        for _ in range(n):
            if not prep:
                return
            prep.pop(0)()
            emit_conv(1)

    def emit_pair_weights(p, wlw):
        wpar = p % 2
        first = [True]
        for wsrc, wdst in ((din["wk"], wkb[wpar]), (din["wv"], wvb[wpar]),
                           (din["wq"], wqb[wpar])):
            for j in range(8):
                P.step("gpsimd", lambda e, s=wsrc[j * 128:(j + 1) * 128, p * 128:(p + 1) * 128],
                       d=wdst[:, j * 128:(j + 1) * 128]: e.dma_start(out=d, in_=s),
                       waits=wlw if first[0] else [], incs=[(qG, 16)])
                first[0] = False
        pstate[p]["qg_w"] = qG.v

    def _hc_buf(j):
        return xnstr[j // 4][:, (j % 4) * 512:(j % 4 + 1) * 512]

    def fill_prep(p, ln_gate=False):
        st = pstate[p]
        st["kmm"] = []
        st["evac_se"] = []
        st["evac_dv"] = []
        kb = K2[p % 2]
        q2b = Q2[p % 2]
        vtb = Vtm[p % 2]
        nmb = nmB[p % 2]
        guard = pstate[p - 1] if p >= 1 else None
        for hh in range(16):
            def loads(hh=hh, st=st, guard=guard):
                for j in range(8):
                    lw = []
                    if j == 0:
                        if st["kmm"]:
                            lw.append((pe, st["kmm"][-1]))
                        elif guard is not None:
                            lw.append((pe, guard["last_qpe"]))
                        if ln_gate:
                            t = 4 * hh + 3
                            lw.append((qA, qa_hist[10 * (t // 8) + t % 8]))
                    P.step("sync", lambda e, b=_hc_buf(j), hh=hh, j=j: e.dma_start(
                        out=b, in_=xnf[hh * 512:(hh + 1) * 512, j * 128:(j + 1) * 128],
                        transpose=True), waits=lw, incs=[(qS, 16)])
                st["qs_hc"] = qS.v

            def mats(hh=hh, st=st, p=p, kb=kb):
                wpar = p % 2

                def mk(e, wpar=wpar):
                    last = None
                    for j in range(8):
                        buf = _hc_buf(j)
                        st_, sp = j == 0, j == 7
                        e.matmul(psB[:, 1024:1536], wkb[wpar][:, j * 128:(j + 1) * 128],
                                 buf, start=st_, stop=sp, skip_group_check=True)
                        last = e.matmul(psB[:, 1536:2048], wvb[wpar][:, j * 128:(j + 1) * 128],
                                        buf, start=st_, stop=sp, skip_group_check=True)
                    return last
                w = [(qS, st["qs_hc"]), (qG, st["qg_w"])]
                if hh == 0:
                    if p > 0:
                        w.append((dv, pstate[p - 1]["dv_nm"]))
                else:
                    w += [(se, st["evac_se"][hh - 1]), (dv, st["evac_dv"][hh - 1])]
                P.step("tensor", mk, waits=w, incs=[(pe, 1)])
                st["kmm"].append(pe.v)
                P.step("scalar", lambda e, hh=hh, kb=kb: e.copy(
                    out=kb[:, hh * 512:(hh + 1) * 512], in_=psB[:, 1024:1536]),
                    waits=[(pe, pe.v)], incs=[(se, 1)])
                st["evac_se"].append(se.v)
                P.step("vector", lambda e, hh=hh: e.tensor_copy(
                    out=V2[:, hh * 512:(hh + 1) * 512], in_=psB[:, 1536:2048]),
                    waits=[(pe, st["kmm"][hh])], incs=[(dv, 1)])
                st["evac_dv"].append(dv.v)
            prep.append(loads)
            prep.append(mats)

        for pp in range(2):
            for j in range(8):
                def qload(pp=pp, j=j, st=st):
                    if j == 0:
                        lw = [(pe, st["kmm"][-1] if pp == 0 else st["qp_pe"])]
                        if ln_gate:
                            lw.append((qA, qa_hist[NJ - 1]))
                    else:
                        lw = [(pe, st["qp_pe"])]
                    buf = xnstr[j % 2][:, 0:1024]
                    P.step("sync", lambda e, b=buf, pp=pp, j=j: e.dma_start(
                        out=b, in_=xnod[pp * 1024:(pp + 1) * 1024, j * 128:(j + 1) * 128],
                        transpose=True), waits=lw, incs=[(qS, 16)])
                    st["qs_q"] = qS.v

                def qmat(pp=pp, j=j, st=st, p=p):
                    wpar = p % 2
                    buf = xnstr[j % 2][:, 0:1024]

                    def mk(e, j=j, buf=buf, wpar=wpar):
                        st_, sp = j == 0, j == 7
                        e.matmul(psB[:, 1024:1536], wqb[wpar][:, j * 128:(j + 1) * 128],
                                 buf[:, 0:512], start=st_, stop=sp, skip_group_check=True)
                        return e.matmul(psB[:, 1536:2048], wqb[wpar][:, j * 128:(j + 1) * 128],
                                        buf[:, 512:1024], start=st_, stop=sp,
                                        skip_group_check=True)
                    w = [(qS, st["qs_q"]), (qG, st["qg_w"])]
                    if j == 0:
                        if pp == 0:
                            w += [(se, st["evac_se"][15]), (dv, st["evac_dv"][15])]
                        else:
                            w.append((se, st["se_qev0"]))
                    P.step("tensor", mk, waits=w, incs=[(pe, 1)])
                    st["qp_pe"] = pe.v
                prep.append(qload)
                prep.append(qmat)

            def qevac(pp=pp, st=st, q2b=q2b):
                qw = [(pe, st["qp_pe"]), (pe, prev_pair["av_pe"])]
                P.step("scalar", lambda e, pp=pp, q2b=q2b: e.activation(
                    out=q2b[:, pp * 1024:(pp + 1) * 1024], in_=psB[:, 1024:2048],
                    func=AF.Copy, scale=0.125), waits=qw, incs=[(se, 1)])
                if pp == 0:
                    st["se_qev0"] = se.v
                else:
                    st["se_q2"] = se.v
                    st["last_qpe"] = st["qp_pe"]
            prep.append(qevac)

        for g in range(16):
            def tp_g(g=g, st=st, vtb=vtb):
                def mk_tp(e, g=g):
                    last = None
                    for jj in range(4):
                        j = g * 4 + jj
                        for h in range(2):
                            last = e.matmul(
                                psB[:, 1024 + h * 512 + jj * 64:
                                    1024 + h * 512 + (jj + 1) * 64],
                                V2[h * 64:(h + 1) * 64, j * 128:(j + 1) * 128],
                                ident[h * 64:(h + 1) * 64, h * 64:(h + 1) * 64],
                                start=True, stop=True, skip_group_check=True)
                    return last
                if g == 0:
                    w = [(se, st["se_q2"]), (pe, prev_pair["av_pe"])]
                else:
                    w = [(dv, st["vtm_dv"])]
                P.step("tensor", mk_tp, waits=w, incs=[(pe, 1)])
                vw = [(pe, pe.v)]
                if g == 0:
                    vw.append((dv, dv_vtones))
                    vw.append((pe, prev_pair["av_pe"]))

                def mk_vcp(e, g=g, vtb=vtb):
                    vt3 = vtb.ap().rearrange("p (b c) -> p b c", c=65)
                    last = None
                    for h in range(2):
                        d = vt3[:, g * 8 + h:(g + 1) * 8:2, 0:64]
                        s = psB[:, 1024 + h * 512:1024 + h * 512 + 256] \
                            .rearrange("p (b c) -> p b c", c=64)
                        last = e.tensor_copy(out=d, in_=s)
                    return last
                P.step("vector", mk_vcp, waits=vw, incs=[(dv, 1)])
                st["vtm_dv"] = dv.v
                if g == 15:
                    st["dv_vtm"] = dv.v
            prep.append(tp_g)

        def d_ksq(st=st, kb=kb):
            P.step("scalar", lambda e, kb=kb: e.activation(
                out=V2[:, 0:S], in_=kb[:], func=AF.Square),
                waits=[(se, st["evac_se"][15]), (dv, st["dv_vtm"])], incs=[(se, 1)])
            st["se_ksq"] = se.v
        prep.append(d_ksq)

        for r in range(8):
            def d_kn(r=r, st=st):
                def mk_kn(e, r=r):
                    e.matmul(psB[0:1, 1024:1536], ones_c1[:],
                             V2[:, r * 1024:r * 1024 + 512],
                             start=True, stop=True, skip_group_check=True)
                    return e.matmul(psB[0:1, 1536:2048], ones_c1[:],
                                    V2[:, r * 1024 + 512:r * 1024 + 1024],
                                    start=True, stop=True, skip_group_check=True)
                w = [(se, st["se_ksq"])]
                w.append((dv, st["dv_vtm"] if r == 0 else st["kn_dv"]))
                P.step("tensor", mk_kn, waits=w, incs=[(pe, 1)])
                P.step("vector", lambda e: e.reduce_max(
                    out=sm1[0:1, 1:2], in_=psB[0:1, 1024:2048], axis=AX.X),
                    waits=[(pe, pe.v)], incs=[(dv, 1)])
                if r == 0:
                    P.step("vector", lambda e: e.tensor_copy(out=sm1[0:1, 0:1],
                                                             in_=sm1[0:1, 1:2]),
                           waits=[(dv, dv.v)], incs=[(dv, 1)])
                else:
                    P.step("vector", lambda e: e.tensor_tensor(
                        out=sm1[0:1, 0:1], in0=sm1[0:1, 0:1], in1=sm1[0:1, 1:2],
                        op=ALU.max), waits=[(dv, dv.v)], incs=[(dv, 1)])
                st["kn_dv"] = dv.v
            prep.append(d_kn)

        def d_qsq(st=st, q2b=q2b):
            P.step("scalar", lambda e, q2b=q2b: e.activation(
                out=V2[:, 0:TOWN], in_=q2b[:], func=AF.Square),
                waits=[(se, st["se_q2"]), (dv, st["kn_dv"])], incs=[(se, 1)])
            st["se_qsq"] = se.v
        prep.append(d_qsq)

        for r in range(2):
            def d_qn(r=r, st=st):
                def mk_qn(e, r=r):
                    e.matmul(psB[0:1, 1024:1536], ones_c1[:],
                             V2[:, r * 1024:r * 1024 + 512],
                             start=True, stop=True, skip_group_check=True)
                    return e.matmul(psB[0:1, 1536:2048], ones_c1[:],
                                    V2[:, r * 1024 + 512:r * 1024 + 1024],
                                    start=True, stop=True, skip_group_check=True)
                w = [(se, st["se_qsq"]), (dv, st["kn_dv"] if r == 0 else st["qn_dv"])]
                P.step("tensor", mk_qn, waits=w, incs=[(pe, 1)])
                P.step("vector", lambda e: e.reduce_max(
                    out=sm1[0:1, 2:3] if r == 0 else sm1[0:1, 6:7],
                    in_=psB[0:1, 1024:2048], axis=AX.X),
                    waits=[(pe, pe.v)], incs=[(dv, 1)])
                if r == 1:
                    P.step("vector", lambda e: e.tensor_tensor(
                        out=sm1[0:1, 2:3], in0=sm1[0:1, 2:3], in1=sm1[0:1, 6:7],
                        op=ALU.max), waits=[(dv, dv.v)], incs=[(dv, 1)])
                st["qn_dv"] = dv.v
            prep.append(d_qn)

        def d_fin(st=st, nmb=nmb):
            P.step("vector", lambda e: e.tensor_tensor(
                out=sm1[0:1, 3:4], in0=sm1[0:1, 2:3], in1=sm1[0:1, 0:1], op=ALU.mult),
                waits=[(dv, st["qn_dv"])], incs=[(dv, 1)])
            dv_m2 = dv.v
            P.step("scalar", lambda e: e.activation(out=sm1[0:1, 4:5], in_=sm1[0:1, 3:4],
                                                    func=AF.Sqrt),
                   waits=[(dv, dv_m2)], incs=[(se, 1)])
            P.step("scalar", lambda e: e.activation(out=sm1[0:1, 5:6], in_=sm1[0:1, 4:5],
                                                    func=AF.Copy, scale=-1.02, bias=-0.5),
                   waits=[(se, se.v)], incs=[(se, 1)])
            nmw = [(se, se.v), (pe, prev_pair["nm_pe"])]
            P.step("scalar", lambda e: e.activation(out=nmb16[:], in_=sm1[0:1, 5:6],
                                                    func=AF.Copy),
                   waits=nmw, incs=[(se, 1)])
            se_nm = se.v
            P.step("tensor", lambda e: e.matmul(psB[0:128, 1024:1025], ones_r128[:],
                                                nmb16[:], start=True, stop=True,
                                                skip_group_check=True),
                   waits=[(se, se_nm), (dv, st["qn_dv"])], incs=[(pe, 1)])
            prev_pair["nm_pe"] = pe.v
            nmcw = [(pe, pe.v), (se, prev_pair["exp_se"])]
            P.step("vector", lambda e, nmb=nmb: e.tensor_copy(out=nmb[:],
                                                              in_=psB[0:128, 1024:1025]),
                   waits=nmcw, incs=[(dv, 1)])
            st["dv_nm"] = dv.v
        prep.append(d_fin)

    def emit_p2f(p):
        st = pstate[p]
        dv_nm = st["dv_nm"]
        dv_vtm = st["dv_vtm"]
        se_q2 = st["se_q2"]
        kb = K2[p % 2]
        q2b = Q2[p % 2]
        vtb = Vtm[p % 2]
        nmb = nmB[p % 2]
        slot_exp_se = []
        slot_av_pe = []
        slot_mm_pe = []
        diag_dv = []
        pend_av = None
        p2g_pend = [None]
        s_idx = 0

        def p2g_head(q, pe_q_last):
            qg = QGI[0]
            stt = {}
            for h in (0, 1):
                P.step("scalar", lambda e, h=h: e.activation(
                    out=rinv[0:1, h * 512:(h + 1) * 512],
                    in_=psB[64:65, h * 512:h * 512 + 512], func=AF.Ln),
                    waits=[(pe, pe_q_last)] if h == 0 else [(se, se.v)], incs=[(se, 1)])
                P.step("scalar", lambda e, h=h: e.activation(
                    out=rinv[0:1, h * 512:(h + 1) * 512],
                    in_=rinv[0:1, h * 512:(h + 1) * 512], func=AF.Exp, scale=-1.0),
                    waits=[(se, se.v)], incs=[(se, 1)])
                stt[h] = se.v
            return (qg, q, stt)

        def p2g_tail(info):
            qg, q, stt = info
            for h in (0, 1):
                P.step("tensor", lambda e, h=h: e.matmul(
                    psB[64:128, h * 512:h * 512 + 512], ones_r64f[:],
                    rinv[0:1, h * 512:(h + 1) * 512],
                    start=True, stop=True, skip_group_check=True),
                    waits=[(se, stt[h])], incs=[(pe, 1)])
                w = [(pe, pe.v)]
                if (qg - 1, h) in r4dv:
                    w.append((dv, r4dv[(qg - 1, h)]))
                P.step("scalar", lambda e, h=h: e.copy(
                    out=rB[:, h * 512:(h + 1) * 512],
                    in_=psB[64:128, h * 512:h * 512 + 512]),
                    waits=w, incs=[(se, 1)])
                sev = se.v
                P.step("vector", lambda e, h=h, q=q: e.tensor_tensor(
                    out=otbuf[h * 64:(h + 1) * 64,
                              p * TOWN + q * 512:p * TOWN + (q + 1) * 512],
                    in0=psB[0:64, h * 512:h * 512 + 512],
                    in1=rB[:, h * 512:(h + 1) * 512], op=ALU.mult),
                    waits=[(se, sev)], incs=[(dv, 1)])
                r4dv[(qg, h)] = dv.v

        for q in range(4):
            qg = QGI[0]
            qlo, qhi = 4 * q, 4 * q + 3
            jmax = 4 * (qhi + 1)
            for j in range(jmax):
                par_s = s_idx % 2
                blo = max(j // 4, qlo)
                pst = psA[:, par_s * 1024:(par_s + 1) * 1024]
                lo = (blo - qlo) * 128
                hi = (qhi - qlo + 1) * 128

                def mk_st(e, j=j, blo=blo, qhi=qhi, pst=pst, lo=lo, hi=hi, kb=kb, q2b=q2b):
                    last = None
                    for h in range(2):
                        last = e.matmul(
                            pst[:, h * 512 + lo:h * 512 + hi],
                            kb[h * 64:(h + 1) * 64, j * 128:(j + 1) * 128],
                            q2b[h * 64:(h + 1) * 64, blo * 128:(qhi + 1) * 128],
                            start=True, stop=True, skip_group_check=True)
                    return last
                w = []
                if s_idx >= 2:
                    w.append((se, slot_exp_se[s_idx - 2]))
                else:
                    w += [(se, se_q2), (dv, dv_nm)]
                P.step("tensor", mk_st, waits=w, incs=[(pe, 1)])
                slot_mm_pe.append(pe.v)

                if j == 1 and p2g_pend[0] is not None:
                    p2g_tail(p2g_pend[0])
                    p2g_pend[0] = None

                if pend_av is not None:
                    aw = list(pend_av[1])
                    if pend_av[2] is not None:
                        aw.append((dv, r4dv[pend_av[2]]))
                    P.step("tensor", pend_av[0], waits=aw, incs=[(pe, 1)])
                    slot_av_pe.append(pe.v)
                    pend_av = None
                    if s_idx % 2 == 1:
                        drain_prep(1)

                ew = [(pe, slot_mm_pe[s_idx]), (dv, dv_nm)]
                if s_idx >= 2:
                    ew.append((pe, slot_av_pe[s_idx - 2]))
                P.step("scalar", lambda e, pst=pst, par_s=par_s, nmb=nmb: e.activation(
                    out=PT[par_s][:], in_=pst, func=AF.Exp, bias=nmb[:]),
                    waits=ew, incs=[(se, 1)])
                slot_exp_se.append(se.v)

                if qlo <= j // 4 <= qhi:
                    b = j // 4
                    srel = j - 4 * b
                    off = (b - qlo) * 128

                    def mk_msk(e, b=b, srel=srel, off=off, par_s=par_s):
                        last = None
                        for h in range(2):
                            sl = PT[par_s][:, h * 512 + off:h * 512 + off + 128]
                            last = e.tensor_tensor(
                                out=sl, in0=sl,
                                in1=maskT[:, (b * 4 + srel) * 128:(b * 4 + srel + 1) * 128],
                                op=ALU.mult)
                        return last
                    P.step("vector", mk_msk, waits=[(se, slot_exp_se[s_idx])], incs=[(dv, 1)])
                    diag_dv.append(dv.v)
                else:
                    diag_dv.append(None)

                def mk_av(e, j=j, jmax=jmax, par_s=par_s, lo=lo, hi=hi, vtb=vtb):
                    last = None
                    for h in range(2):
                        last = e.matmul(
                            psB[0:65, h * 512 + lo:h * 512 + hi],
                            vtb[:, (j * 2 + h) * 65:(j * 2 + h) * 65 + 65],
                            PT[par_s][:, h * 512 + lo:h * 512 + hi],
                            start=(j == 0), stop=(j == jmax - 1), skip_group_check=True)
                    return last
                avw = [(se, slot_exp_se[s_idx])]
                reg_free = None
                if diag_dv[s_idx] is not None:
                    avw.append((dv, diag_dv[s_idx]))
                if j == 0:
                    if q == 0:
                        avw.append((dv, dv_nm))
                    else:
                        reg_free = (qg - 1, 1)
                    avw.append((dv, dv_vtm))
                pend_av = (mk_av, avw, reg_free)
                s_idx += 1
            P.step("tensor", pend_av[0], waits=pend_av[1], incs=[(pe, 1)])
            slot_av_pe.append(pe.v)
            pend_av = None
            p2g_pend[0] = p2g_head(q, pe.v)
            QGI[0] += 1
        p2g_tail(p2g_pend[0])
        p2g_pend[0] = None
        prev_pair.update({
            "exp_se": slot_exp_se[-1],
            "av_pe": slot_av_pe[-1],
            "ot_dv": r4dv[(QGI[0] - 1, 1)],
        })

    emit_pair_weights(0, [])
    emit_pair_weights(1, [])
    fill_prep(0, ln_gate=True)

    for T in range(8):
        for k in range(8):
            i = 8 * T + k
            ln_jobs.append((din["xk"][i * 128:(i + 1) * 128, :],
                            xnf[i * 128:(i + 1) * 128, :]))
        for k in range(2):
            i = 2 * T + k
            ln_jobs.append((din["xo"][i * 128:(i + 1) * 128, :],
                            xnod[i * 128:(i + 1) * 128, :]))
    NJ = len(ln_jobs)
    chunks_done = [0]

    def maybe_chunk(i_store):
        while chunks_done[0] < 8:
            T = chunks_done[0]
            trigger = 10 * (T + 1) + 7 if T < 7 else NJ - 1
            if i_store != trigger:
                break
            emit_conv(2)
            drain_prep(4)
            chunks_done[0] += 1
            if T < 7:
                break
    for i in range(NJ + 2):
        if i < NJ:
            ln_stage_load_A(i)
        if 1 <= i <= NJ:
            ln_stage_BCD(i - 1)
        if i >= 2:
            ln_stage_E(i - 2)
            maybe_chunk(i - 2)
    drain_prep(len(prep))
    for p in range(NPAIR):
        if p + 1 < NPAIR:
            if p >= 1:
                emit_pair_weights(p + 1, [(pe, pstate[p - 1]["last_qpe"])])
            fill_prep(p + 1)
        emit_p2f(p)
        drain_prep(len(prep))

    emit_conv(len(conv_jobs))
    qg_conv_all = qG.v
    dv_ot_final = prev_pair["ot_dv"]
    if stop_after == "P2":
        return _early_exit("P2")

    p2s.close()
    wob = sb(pOTs, "wob", [128, 8 * 1024], BF16)
    y1_sb = [sb(pOTs, f"y1sb{i}", [128, D], F32) for i in range(2)]
    for pch in range(8):
        P.step("gpsimd", lambda e, pch=pch: e.dma_start(
            out=wob[:, pch * 1024:(pch + 1) * 1024],
            in_=din["wo"][pch * 128:(pch + 1) * 128, :]),
            waits=[(qG, qg_conv_all), (dv, dv_ot_final)] if pch == 0 else [],
            incs=[(qG, 16)])
    qg_wob = qG.v

    h_setf = []
    h_dvstats = []
    h_dvy1 = []
    h_qaxn2 = []
    h_sexn2 = []
    h_dvy1fin = []
    h_qgy1 = []
    for i in range(16):
        par = i % 2
        reg = psA[:, par * 1024:(par + 1) * 1024]
        lw = [(dv, h_dvy1[i - 2])] if i >= 2 else [(dv, dv_ot_final)]
        P.step("sync", lambda e, i=i, par=par: e.dma_start(
            out=xt[par][:], in_=din["xo"][i * 128:(i + 1) * 128, :]),
            waits=lw, incs=[(qS, 16)])
        qs_xo = qS.v

        def mk_wo(e, i=i, reg=reg):
            last = None
            for pch in range(8):
                st, sp = pch == 0, pch == 7
                lhsT = otbuf[:, pch * TOWN + i * 128:pch * TOWN + (i + 1) * 128]
                e.matmul(reg[:, 0:512], lhsT, wob[:, pch * 1024:pch * 1024 + 512],
                         start=st, stop=sp, skip_group_check=True)
                last = e.matmul(reg[:, 512:1024], lhsT, wob[:, pch * 1024 + 512:(pch + 1) * 1024],
                                start=st, stop=sp, skip_group_check=True)
            return last
        w = [(qG, qg_wob), (dv, dv_ot_final)]
        if i >= 2:
            w += [(se, h_setf[i - 2]), (dv, h_dvstats[i - 2])]
        P.step("tensor", mk_wo, waits=w, incs=[(pe, 1)])
        pe_wo = pe.v

        c = tuple(st_sm[:, k + 8 * par:k + 8 * par + 1] for k in range(4))
        dv_fin, dv_stats = emit_ln(reg, sqt[par][:], c,
                                   sq_waits=[(pe, pe_wo)], stats_waits=[(pe, pe_wo)])
        tw = [(dv, dv_fin)]
        if i >= 1:
            tw.append((dv, h_dvy1fin[i - 1]))
        P.step("scalar", lambda e, reg=reg, c=c: e.activation(
            out=t_sb[:], in_=reg, func=AF.Identity, bias=c[3], scale=c[2]),
            waits=tw, incs=[(se, 1)])
        h_setf.append(se.v)
        h_dvstats.append(dv_stats)
        yw = [(se, se.v), (qS, qs_xo)]
        if i >= 2:
            yw.append((qG, h_qgy1[i - 2]))
        P.step("vector", lambda e, par=par: e.tensor_add(
            out=y1_sb[par][:], in0=t_sb[:], in1=xt[par][:]),
            waits=yw, incs=[(dv, 1)])
        h_dvy1.append(dv.v)
        h_dvy1fin.append(dv.v)
        P.step("gpsimd", lambda e, i=i, par=par: e.dma_start(
            out=y1d[i * 128:(i + 1) * 128, :], in_=y1_sb[par][:]),
            waits=[(dv, dv.v)], incs=[(qG, 16)])
        h_qgy1.append(qG.v)

        dv_fin2, _ = emit_ln(y1_sb[par][:], sqt[par][:], c,
                             sq_waits=[(dv, h_dvy1[i])], stats_waits=[])
        nw = [(dv, dv_fin2)]
        if i >= 2:
            nw.append((qA, h_qaxn2[i - 2]))
        P.step("scalar", lambda e, par=par, c=c: e.activation(
            out=xnt[par][:], in_=y1_sb[par][:], func=AF.Identity, bias=c[3], scale=c[2]),
            waits=nw, incs=[(se, 1)])
        h_sexn2.append(se.v)
        P.step("scalar", lambda e, i=i, par=par: e.dma_start(
            out=xn2d[i * 128:(i + 1) * 128, :], in_=xnt[par][:]),
            waits=[(se, h_sexn2[-1])], incs=[(qA, 16)])
        h_qaxn2.append(qA.v)
    qa_p3a = qA.v
    qg_p3a = qG.v
    if stop_after == "P3a":
        return _early_exit("P3a")
    se_p3a = se.v
    dv_p3a = dv.v

    if debug:
        P.step("gpsimd", lambda e: e.dma_start(out=dbg["y1"], in_=y1d),
               waits=[(qG, qg_p3a), (qA, qa_p3a)], incs=[(qG, 16)])
        P.step("gpsimd", lambda e: e.dma_start(out=dbg["xn2"], in_=xn2d),
               waits=[(qG, qg_p3a), (qA, qa_p3a)], incs=[(qG, 16)])

    pOTs.close()
    out_sb = [sb(es, f"outsb{i}", [128, D], F32) for i in range(2)]
    A_buf = sb(es, "A_buf", [128, 32 * 512], BF16)
    B_buf = sb(es, "B_buf", [128, 32 * 512], BF16)
    xn2f = sb(es, "xn2f", [128, 8 * 512], BF16)
    w12s = [sb(es, f"w12s{i}", [128, 2 * 8 * 128], BF16) for i in range(4)]
    hstr = [sb(es, f"hstr{i}", [128, 32 * 128], BF16) for i in range(2)]
    w3s = [sb(es, f"w3s{i}", [128, 1024], BF16) for i in range(4)]
    hA = [sb(es, f"hA{i}", [128, 512], F32) for i in range(2)]
    hG = [sb(es, f"hG{i}", [128, 512], F32) for i in range(2)]
    f_sb = [sb(es, f"fsb{i}", [128, D], F32) for i in range(2)]

    n_out = 0
    prev_f_dv = dv_p3a
    prev_w1_pe = pe.v
    h_dvoutfin = []
    h_qout = []
    h_se_t3 = []
    gidx = 0
    for T in range(4):
        t0 = T * 512
        for j in range(8):
            w = []
            if T == 0 and j == 0:
                w.append((qA, qa_p3a))
            if T > 0 and j == 0:
                w.append((pe, prev_w1_pe))
            P.step("sync", lambda e, j=j, t0=t0: e.dma_start(
                out=xn2f[:, j * 512:(j + 1) * 512],
                in_=xn2d[t0:t0 + 512, j * 128:(j + 1) * 128], transpose=True),
                waits=w, incs=[(qS, 16)])
        qs_xn2f = qS.v

        w12_pe = {}
        w12_se = {}
        w12_dv = {}
        for m in range(32):
            par = m % 4
            par2 = m % 2
            lw = [(pe, w12_pe[m - 4])] if m >= 4 else []
            P.step("sync", lambda e, m=m, par=par: e.dma_start(
                out=w12s[par][:, 0:1024], in_=wc1[m]), waits=lw, incs=[(qS, 16)])
            P.step("sync", lambda e, m=m, par=par: e.dma_start(
                out=w12s[par][:, 1024:2048], in_=wc2[m]), incs=[(qS, 16)])
            qs_w = qS.v
            bank = psA if par < 2 else psB
            ps1 = bank[:, (par % 2) * 512:(par % 2) * 512 + 512]
            ps2 = bank[:, 1024 + (par % 2) * 512:1024 + (par % 2) * 512 + 512]

            def mk12(e, par=par, ps1=ps1, ps2=ps2):
                last = None
                for j in range(8):
                    st, sp = j == 0, j == 7
                    rhs = xn2f[:, j * 512:(j + 1) * 512]
                    e.matmul(ps1, w12s[par][:, j * 128:(j + 1) * 128], rhs,
                             start=st, stop=sp, skip_group_check=True)
                    last = e.matmul(ps2, w12s[par][:, 1024 + j * 128:1024 + (j + 1) * 128], rhs,
                                    start=st, stop=sp, skip_group_check=True)
                return last
            w = [(qS, qs_w)]
            if m >= 4:
                w += [(se, w12_se[m - 4]), (dv, w12_dv[m - 4])]
            else:
                w.append((dv, prev_f_dv))
                if T == 0:
                    w += [(se, se_p3a), (dv, dv_p3a), (qG, qg_p3a)]
            P.step("tensor", mk12, waits=w, incs=[(pe, 1)])
            w12_pe[m] = pe.v
            P.step("scalar", lambda e, ps1=ps1, m=m, par2=par2: e.activation(
                out=hA[par2][:], in_=ps1, func=AF.Silu, scale=scl_s1[:, m:m + 1]),
                waits=[(pe, pe.v)] + ([(dv, w12_dv[m - 2])] if m >= 2 else []),
                incs=[(se, 1)])
            w12_se[m] = se.v
            P.step("vector", lambda e, ps2=ps2, m=m, par2=par2: e.tensor_scalar(
                out=hG[par2][:], in0=ps2, scalar1=scl_s2[:, m:m + 1], scalar2=None, op0=ALU.mult),
                waits=[(pe, w12_pe[m])], incs=[(dv, 1)])
            P.step("vector", lambda e, m=m, par2=par2: e.tensor_tensor(
                out=A_buf[:, m * 512:(m + 1) * 512], in0=hA[par2][:], in1=hG[par2][:],
                op=ALU.mult),
                waits=[(se, w12_se[m])], incs=[(dv, 1)])
            w12_dv[m] = dv.v
        dv_A = dv.v
        prev_w1_pe = w12_pe[31]
        if debug and T == 0:
            P.step("gpsimd", lambda e: e.dma_start(out=dbg["A0"], in_=A_buf[:]),
                   waits=[(dv, dv_A)], incs=[(qG, 16)])
            dv_A = dv.v
            h1_dbg_qg = qG.v

        h1_pe = {}
        h1_se = {}
        for m in range(32):
            par = m % 4
            lw = [(pe, h1_pe[m - 2])] if m >= 2 else [(pe, w12_pe[30 + m])]
            P.step("sync", lambda e, m=m, par=par % 2: e.dma_start(
                out=hstr[par][:], in_=wch1[m]), waits=lw, incs=[(qS, 16)])
            qs_h = qS.v
            bank = psA if par < 2 else psB
            ps1 = bank[:, (par % 2) * 512:(par % 2) * 512 + 512]

            def mkh1(e, par=par % 2, ps1=ps1):
                last = None
                for j in range(32):
                    last = e.matmul(ps1, hstr[par][:, j * 128:(j + 1) * 128],
                                    A_buf[:, j * 512:(j + 1) * 512],
                                    start=(j == 0), stop=(j == 31), skip_group_check=True)
                return last
            w = [(qS, qs_h), (dv, dv_A)]
            if m >= 4:
                w.append((se, h1_se[m - 4]))
            else:
                w.append((se, w12_se[28 + m]))
            P.step("tensor", mkh1, waits=w, incs=[(pe, 1)])
            h1_pe[m] = pe.v
            P.step("scalar", lambda e, ps1=ps1, m=m: e.activation(
                out=B_buf[:, m * 512:(m + 1) * 512], in_=ps1, func=AF.Gelu,
                scale=scl_h1[:, m:m + 1]),
                waits=[(pe, pe.v)], incs=[(se, 1)])
            h1_se[m] = se.v
        se_B = se.v
        if debug and T == 0:
            P.step("gpsimd", lambda e: e.dma_start(out=dbg["B0"], in_=B_buf[:]),
                   waits=[(se, se_B)], incs=[(qG, 16)])

        h2_pe = {}
        h2_se = {}
        for m in range(32):
            par = m % 4
            lw = [(pe, h2_pe[m - 2])] if m >= 2 else [(pe, h1_pe[30 + m])]
            P.step("sync", lambda e, m=m, par=par % 2: e.dma_start(
                out=hstr[par][:], in_=wch2[m]), waits=lw, incs=[(qS, 16)])
            qs_h = qS.v
            bank = psA if par < 2 else psB
            ps1 = bank[:, (par % 2) * 512:(par % 2) * 512 + 512]

            def mkh2(e, par=par % 2, ps1=ps1):
                last = None
                for j in range(32):
                    last = e.matmul(ps1, hstr[par][:, j * 128:(j + 1) * 128],
                                    B_buf[:, j * 512:(j + 1) * 512],
                                    start=(j == 0), stop=(j == 31), skip_group_check=True)
                return last
            w = [(qS, qs_h), (se, se_B)]
            if m >= 4:
                w.append((se, h2_se[m - 4]))
            else:
                w.append((se, h1_se[28 + m]))
            P.step("tensor", mkh2, waits=w, incs=[(pe, 1)])
            h2_pe[m] = pe.v
            h2w = [(pe, pe.v)]
            if debug and T == 0 and m < 2:
                h2w.append((qG, h1_dbg_qg))
            P.step("scalar", lambda e, ps1=ps1, m=m: e.activation(
                out=A_buf[:, m * 512:(m + 1) * 512], in_=ps1, func=AF.Gelu,
                scale=scl_h2[:, m:m + 1]),
                waits=h2w, incs=[(se, 1)])
            h2_se[m] = se.v
        se_A2 = se.v
        if debug and T == 0:
            P.step("gpsimd", lambda e: e.dma_start(out=dbg["A2"], in_=A_buf[:]),
                   waits=[(se, se_A2)], incs=[(qG, 16)])

        w3_pe = {}
        for j in range(32):
            par = j % 4
            lw = [(pe, w3_pe[j - 4])] if j >= 4 else []
            P.step("sync", lambda e, j=j, par=par: e.dma_start(
                out=w3s[par][:], in_=wc3[j * 128:(j + 1) * 128, :]),
                waits=lw, incs=[(qS, 16)])
            qs_w3 = qS.v

            def mkw3(e, j=j, par=par):
                last = None
                for sI in range(4):
                    ps = psA if sI < 2 else psB
                    regw = ps[:, (sI % 2) * 1024:(sI % 2) * 1024 + 1024]
                    lhsT = A_buf[:, j * 512 + sI * 128:j * 512 + (sI + 1) * 128]
                    e.matmul(regw[:, 0:512], lhsT, w3s[par][:, 0:512],
                             start=(j == 0), stop=(j == 31), skip_group_check=True)
                    last = e.matmul(regw[:, 512:1024], lhsT, w3s[par][:, 512:1024],
                                    start=(j == 0), stop=(j == 31), skip_group_check=True)
                return last
            w = [(qS, qs_w3), (se, se_A2)]
            if j == 0:
                w.append((dv, prev_f_dv))
                if T == 0:
                    w.append((dv, dv_p3a))
            P.step("tensor", mkw3, waits=w, incs=[(pe, 1)])
            w3_pe[j] = pe.v
        pe_w3 = pe.v

        for sI in range(4):
            par = sI % 2
            ps = psA if sI < 2 else psB
            regw = ps[:, (sI % 2) * 1024:(sI % 2) * 1024 + 1024]
            fw = [(pe, pe_w3), (dv, dv_s3b)]
            if gidx >= 2:
                fw.append((se, h_se_t3[gidx - 2]))
                if debug and T == 0:
                    fw.append((qG, f_dbg_qg))
            P.step("vector", lambda e, regw=regw, par=par: e.tensor_tensor(
                out=f_sb[par][:], in0=regw, in1=s3B[:], op=ALU.mult),
                waits=fw, incs=[(dv, 1)])
            P.step("vector", lambda e: e.tensor_copy(out=st_sm[0:1, 14:15], in_=st_sm[0:1, 14:15]),
                   incs=[(dv, 1)])
            dv_f = dv.v
            if sI == 3:
                psum_free_dv = dv_f
            if debug and T == 0:
                P.step("gpsimd", lambda e, sI=sI, par=par: e.dma_start(
                    out=dbg["F0"][:, sI * 1024:(sI + 1) * 1024], in_=f_sb[par][:]),
                    waits=[(dv, dv_f)], incs=[(qG, 16)])
                f_dbg_qg = qG.v
            lw = [(dv, h_dvoutfin[gidx - 2])] if gidx >= 2 else [(dv, h_dvy1[14 + gidx])]
            P.step("sync", lambda e, t0=t0, sI=sI, par=par: e.dma_start(
                out=xt[par][:], in_=y1d[t0 + sI * 128:t0 + (sI + 1) * 128, :]),
                waits=lw, incs=[(qS, 16)])
            qs_y1 = qS.v
            c = tuple(st_sm[:, k + 8 * par:k + 8 * par + 1] for k in range(4))
            dv_fin3, _ = emit_ln(f_sb[par][:], sqt[par][:], c,
                                 sq_waits=[(dv, dv_f)], stats_waits=[])
            tw = [(dv, dv_fin3)]
            if gidx >= 1:
                tw.append((dv, h_dvoutfin[gidx - 1]))
            P.step("scalar", lambda e, par=par, c=c: e.activation(
                out=t_sb[:], in_=f_sb[par][:], func=AF.Identity, bias=c[3], scale=c[2]),
                waits=tw, incs=[(se, 1)])
            P.step("scalar", lambda e: e.activation(out=st_sm[0:1, 15:16],
                                                    in_=st_sm[0:1, 15:16], func=AF.Copy),
                   incs=[(se, 1)])
            h_se_t3.append(se.v)
            if debug and T == 0:
                P.step("gpsimd", lambda e, sI=sI: e.dma_start(
                    out=dbg["T0"][:, sI * 1024:(sI + 1) * 1024], in_=t_sb[:]),
                    waits=[(se, se.v)], incs=[(qG, 16)])
                P.step("gpsimd", lambda e, sI=sI, par=par: e.dma_start(
                    out=dbg["X0"][:, sI * 1024:(sI + 1) * 1024], in_=xt[par][:]),
                    waits=[(qS, qs_y1)], incs=[(qG, 16)])
                t_dbg_qg = qG.v
            ow = [(se, se.v), (qS, qs_y1)]
            if debug and T == 0:
                ow.append((qG, t_dbg_qg))
            if gidx >= 2:
                ow.append((st_out, 16 * (gidx - 1)))
            P.step("vector", lambda e, par=par: e.tensor_add(
                out=out_sb[par][:], in0=t_sb[:], in1=xt[par][:]),
                waits=ow, incs=[(dv, 1)])
            P.step("vector", lambda e: e.tensor_copy(out=st_sm[0:1, 14:15], in_=st_sm[0:1, 14:15]),
                   incs=[(dv, 1)])
            h_dvoutfin.append(dv.v)
            P.step("gpsimd", lambda e, t0=t0, sI=sI, par=par: e.dma_start(
                out=out[t0 + sI * 128:t0 + (sI + 1) * 128, :], in_=out_sb[par][:]),
                waits=[(dv, dv.v)], incs=[(st_out, 16)])
            h_qout.append(st_out.v)
            n_out += 1
            gidx += 1
        prev_f_dv = psum_free_dv

    P.step("gpsimd", lambda e: e.engine_nop(), waits=[(st_out, 16 * n_out)])

    with es:
        block = es.enter_context(nc.Block())
        P.emit(block)
    return nc


def _tiles_for_core(c):
    cp = c % 4
    return [4 * b + ((cp + b) % 4) for b in range(NL)]


def _stage(inputs):
    x = np.ascontiguousarray(np.asarray(inputs["L0_attn_x"], dtype=np.float32))
    w = {k: np.ascontiguousarray(np.asarray(v, dtype=np.float32)) for k, v in inputs.items()
         if k != "L0_attn_x"}
    shared = dict(
        wq=w["L0_attn_WQ"], wk=w["L0_attn_WK"], wv=w["L0_attn_WV"], wo=w["L0_attn_WO"],
        w1t=np.ascontiguousarray(w["L0_ffn_w1_w"].T),
        w2t=np.ascontiguousarray(w["L0_ffn_w2_w"].T),
        h1t=np.ascontiguousarray(w["L0_ffn_h1_w"].T),
        h2t=np.ascontiguousarray(w["L0_ffn_h2_w"].T),
        w3t=np.ascontiguousarray(w["L0_ffn_w3_w"].T),
        s1c=np.ascontiguousarray(w["L0_ffn_w1_s"].reshape(32, 128).T),
        s2c=np.ascontiguousarray(w["L0_ffn_w2_s"].reshape(32, 128).T),
        hs1c=np.ascontiguousarray(w["L0_ffn_h1_s"].reshape(32, 128).T),
        hs2c=np.ascontiguousarray(w["L0_ffn_h2_s"].reshape(32, 128).T),
        s3r=np.ascontiguousarray(w["L0_ffn_w3_s"].reshape(1, D)),
        identb=np.eye(128, dtype=ml_dtypes.bfloat16),
        ones_c1=np.ones((128, 1), dtype=ml_dtypes.bfloat16),
        ones_r128=np.ones((1, 128), dtype=ml_dtypes.bfloat16),
        ones_r64f=np.ones((1, 64), dtype=np.float32),
        ones_r128f=np.ones((1, 128), dtype=np.float32),
        epsb=np.full((128, 1), EPS, dtype=np.float32),
    )
    maps = []
    pidx = np.arange(128)[:, None]
    qidx = np.arange(128)[None, :]
    for c in range(8):
        g = c // 4
        tiles = _tiles_for_core(c)
        xo = np.concatenate([x[g, t * 128:(t + 1) * 128, :] for t in tiles], axis=0)
        maskT = np.zeros((128, NL * 4 * 128), dtype=np.float32)
        for b in range(NL):
            tb = tiles[b]
            for sblk in range(4):
                kglob = 512 * b + 128 * sblk + pidx
                qglob = tb * 128 + qidx
                maskT[:, (b * 4 + sblk) * 128:(b * 4 + sblk + 1) * 128] = (kglob <= qglob)
        m = dict(shared)
        m["xk"] = np.ascontiguousarray(x[g])
        m["xo"] = np.ascontiguousarray(xo)
        m["maskT"] = maskT.astype(ml_dtypes.bfloat16)
        maps.append(m)
    return maps


def kernel(**inputs):
    maps = _stage(inputs)
    nc = build_nc(debug=False)
    res = run_bass_kernel_spmd(nc, maps, core_ids=list(range(8)))
    y = np.empty((BATCH, S, D), dtype=np.float32)
    for c in range(8):
        g = c // 4
        o = np.asarray(res.results[c]["out"], dtype=np.float32)
        for b, t in enumerate(_tiles_for_core(c)):
            y[g, t * 128:(t + 1) * 128, :] = o[b * 128:(b + 1) * 128, :]
    return y


# version 13
# speedup vs baseline: 1.2100x; 1.1297x over previous
import sys

sys.path.insert(0, "/opt/trn_rl_repo")
import contextlib

import numpy as np
import ml_dtypes

import concourse.bass as bass
import concourse.mybir as mybir
from concourse.bass_utils import run_bass_kernel_spmd

F32 = mybir.dt.float32
BF16 = mybir.dt.bfloat16
AF = mybir.ActivationFunctionType
ALU = mybir.AluOpType
AX = mybir.AxisListType

D = 1024
S = 8192
BATCH = 2
NL = 16
TOWN = NL * 128
NPAIR = 8
EPS = 1e-5


class Sem:
    def __init__(self, handle):
        self.h = handle
        self.v = 0


class Prog:
    def __init__(self):
        self.ops = []

    def step(self, eng, builder, waits=(), incs=()):
        waits = tuple((s, v) for s, v in waits)
        incs = tuple(incs)
        for s, n in incs:
            s.v += n

        def fn(e, builder=builder, waits=waits, incs=incs):
            for sem, val in waits:
                if val > 0:
                    e.wait_ge(sem.h, val)
            inst = builder(e)
            for sem, n in incs:
                inst.then_inc(sem.h, n)

        self.ops.append((eng, fn))

    def emit(self, block):
        for name in ("sync", "gpsimd", "vector", "scalar", "tensor"):
            ops = [fn for eng, fn in self.ops if eng == name]
            if not ops:
                continue

            def make(ops):
                def run(e):
                    for fn in ops:
                        fn(e)
                return run

            getattr(block, name)(make(ops))


def build_nc(debug=False, stop_after=None):
    nc = bass.Bass("TRN2", target_bir_lowering=False)

    din = {}
    for name, shape, dt in [
        ("xk", [S, D], F32), ("xo", [TOWN, D], F32),
        ("wq", [D, D], F32), ("wk", [D, D], F32), ("wv", [D, D], F32), ("wo", [D, D], F32),
        ("w1t", [D, 4096], F32), ("w2t", [D, 4096], F32),
        ("h1t", [4096, 4096], F32), ("h2t", [4096, 4096], F32),
        ("w3t", [4096, D], F32),
        ("s1c", [128, 32], F32), ("s2c", [128, 32], F32),
        ("hs1c", [128, 32], F32), ("hs2c", [128, 32], F32),
        ("s3r", [1, D], F32),
        ("maskT", [128, NL * 4 * 128], BF16),
        ("identb", [128, 128], BF16),
        ("ones_c1", [128, 1], BF16),
        ("ones_r128", [1, 128], BF16),
        ("ones_r64f", [1, 64], F32),
        ("ones_r128f", [1, 128], F32),
        ("epsb", [128, 1], F32),
    ]:
        din[name] = nc.dram_tensor(name, shape, dt, kind="ExternalInput").ap()

    out = nc.dram_tensor("out", [TOWN, D], F32, kind="ExternalOutput").ap()
    dbg = {}
    if debug:
        dbg["xnf"] = nc.dram_tensor("dbg_xnf", [S, D], BF16, kind="ExternalOutput").ap()
        dbg["ot"] = nc.dram_tensor("dbg_ot", [128, NPAIR * TOWN], BF16, kind="ExternalOutput").ap()
        dbg["y1"] = nc.dram_tensor("dbg_y1", [TOWN, D], F32, kind="ExternalOutput").ap()
        dbg["A0"] = nc.dram_tensor("dbg_A0", [128, 32 * 512], BF16, kind="ExternalOutput").ap()
        dbg["B0"] = nc.dram_tensor("dbg_B0", [128, 32 * 512], BF16, kind="ExternalOutput").ap()
        dbg["A2"] = nc.dram_tensor("dbg_A2", [128, 32 * 512], BF16, kind="ExternalOutput").ap()
        dbg["F0"] = nc.dram_tensor("dbg_F0", [128, 4 * 1024], F32, kind="ExternalOutput").ap()
        dbg["T0"] = nc.dram_tensor("dbg_T0", [128, 4 * 1024], F32, kind="ExternalOutput").ap()
        dbg["X0"] = nc.dram_tensor("dbg_X0", [128, 4 * 1024], F32, kind="ExternalOutput").ap()
        dbg["xn2"] = nc.dram_tensor("dbg_xn2", [TOWN, D], BF16, kind="ExternalOutput").ap()

    xnf = nc.dram_tensor("xnf", [S, D], BF16, kind="Internal").ap()
    xnod = nc.dram_tensor("xnod", [TOWN, D], BF16, kind="Internal").ap()
    y1d = nc.dram_tensor("y1d", [TOWN, D], F32, kind="Internal").ap()
    xn2d = nc.dram_tensor("xn2d", [TOWN, D], BF16, kind="Internal").ap()
    wc1 = nc.dram_tensor("wc1", [32, 128, 8 * 128], BF16, kind="Internal").ap()
    wc2 = nc.dram_tensor("wc2", [32, 128, 8 * 128], BF16, kind="Internal").ap()
    wch1 = nc.dram_tensor("wch1", [32, 128, 32 * 128], BF16, kind="Internal").ap()
    wch2 = nc.dram_tensor("wch2", [32, 128, 32 * 128], BF16, kind="Internal").ap()
    wc3 = nc.dram_tensor("wc3", [4096, D], BF16, kind="Internal").ap()

    es = contextlib.ExitStack()
    pOTs = contextlib.ExitStack()
    p2s = contextlib.ExitStack()

    def sb(stack, name, shape, dt):
        return stack.enter_context(nc.sbuf_tensor(name, shape, dt))

    s3B = sb(es, "s3B", [128, D], F32)
    ident = sb(es, "ident", [128, 128], BF16)
    ones_c1 = sb(es, "ones_c1s", [128, 1], BF16)
    ones_r128 = sb(es, "ones_r128s", [1, 128], BF16)
    ones_r64f = sb(es, "ones_r64fs", [1, 64], F32)
    ones_r128f = sb(es, "ones_r128fs", [1, 128], F32)
    epsB = sb(es, "epsB", [128, 1], F32)
    scl_s1 = sb(es, "scl_s1", [128, 32], F32)
    scl_s2 = sb(es, "scl_s2", [128, 32], F32)
    scl_h1 = sb(es, "scl_h1", [128, 32], F32)
    scl_h2 = sb(es, "scl_h2", [128, 32], F32)
    s3r_sb = sb(es, "s3r_sb", [1, D], F32)
    xt = [sb(es, f"xt{i}", [128, D], F32) for i in range(4)]
    sqt = [sb(es, f"sqt{i}", [128, D], F32) for i in range(2)]
    xnt = [sb(es, f"xnt{i}", [128, D], BF16) for i in range(4)]
    st_sm = sb(es, "st_sm", [128, 16], F32)
    t_sb = sb(es, "t_sb", [128, D], F32)

    otbuf = sb(pOTs, "otbuf", [128, NPAIR * TOWN], BF16)

    K2 = [sb(p2s, f"K2{i}", [128, S], BF16) for i in range(2)]
    V2 = sb(p2s, "V2", [128, S], BF16)
    Vtm = sb(p2s, "Vtm", [128, 64 * 2 * 65], BF16)
    Q2 = sb(p2s, "Q2", [128, TOWN], BF16)
    maskT = sb(p2s, "maskT_s", [128, NL * 4 * 128], BF16)
    xnstr = [sb(p2s, f"xnstr{i}", [128, 2048], BF16) for i in range(2)]
    wkb = [sb(p2s, f"wkb{i}", [128, 8 * 128], BF16) for i in range(2)]
    wvb = [sb(p2s, f"wvb{i}", [128, 8 * 128], BF16) for i in range(2)]
    wqb = [sb(p2s, f"wqb{i}", [128, 8 * 128], BF16) for i in range(2)]
    PT = [sb(p2s, f"PT{i}", [128, 1024], BF16) for i in range(2)]
    nmB = sb(p2s, "nmB", [128, 1], F32)
    nmb16 = sb(p2s, "nmb16", [1, 1], BF16)
    sm1 = sb(p2s, "sm1", [1, 8], F32)
    rinv = sb(p2s, "rinv", [1, 1024], F32)
    rB = sb(p2s, "rB", [64, 1024], F32)
    conv = [sb(p2s, f"conv{i}", [128, 16 * 128], BF16) for i in range(2)]

    psA = es.enter_context(nc.psum_tensor("psA", [128, 2048], F32))
    psB = es.enter_context(nc.psum_tensor("psB", [128, 2048], F32))

    qS = Sem(es.enter_context(nc.semaphore("qS")))
    qG = Sem(es.enter_context(nc.semaphore("qG")))
    qA = Sem(es.enter_context(nc.semaphore("qA")))
    pe = Sem(es.enter_context(nc.semaphore("pe")))
    se = Sem(es.enter_context(nc.semaphore("se")))
    dv = Sem(es.enter_context(nc.semaphore("dv")))
    st_out = Sem(es.enter_context(nc.semaphore("st_out")))

    P = Prog()

    for dst, src in [
        (s3r_sb[:], din["s3r"]), (ident[:], din["identb"]),
        (ones_c1[:], din["ones_c1"]), (ones_r128[:], din["ones_r128"]),
        (ones_r64f[:], din["ones_r64f"]), (ones_r128f[:], din["ones_r128f"]),
        (epsB[:], din["epsb"]),
        (scl_s1[:], din["s1c"]), (scl_s2[:], din["s2c"]),
        (scl_h1[:], din["hs1c"]), (scl_h2[:], din["hs2c"]),
        (maskT[:], din["maskT"]),
    ]:
        P.step("sync", lambda e, d=dst, s=src: e.dma_start(out=d, in_=s), incs=[(qS, 16)])
    qs_consts = qS.v

    vt_ones = Vtm.ap().rearrange("p (b c) -> p b c", c=65)[:, :, 64:65]
    P.step("vector", lambda e: e.memset(vt_ones, 1.0), incs=[(dv, 1)])
    dv_vtones = dv.v

    def mk_s3b(e):
        e.matmul(psA[:, 0:512], ones_r128f[:], s3r_sb[0:1, 0:512],
                 start=True, stop=True, skip_group_check=True)
        return e.matmul(psA[:, 512:1024], ones_r128f[:], s3r_sb[0:1, 512:1024],
                        start=True, stop=True, skip_group_check=True)
    P.step("tensor", mk_s3b, waits=[(qS, qs_consts)], incs=[(pe, 1)])
    P.step("vector", lambda e: e.tensor_copy(out=s3B[:], in_=psA[:, 0:1024]),
           waits=[(pe, pe.v)], incs=[(dv, 1)])
    dv_s3b = dv.v

    def _early_exit(tag):
        P.step("gpsimd", lambda e: e.dma_start(out=out[0:128, :], in_=s3B[:]),
               waits=[(dv, dv.v), (se, se.v), (qS, qS.v), (qG, qG.v), (qA, qA.v)],
               incs=[(st_out, 16)])
        P.step("gpsimd", lambda e: e.engine_nop(), waits=[(st_out, st_out.v)])
        for stk in (p2s, pOTs):
            try:
                stk.close()
            except Exception:
                pass
        with es:
            block = es.enter_context(nc.Block())
            P.emit(block)
        return nc

    conv_jobs = []
    for m in range(32):
        conv_jobs.append((din["w1t"][:, m * 128:(m + 1) * 128]
                          .rearrange("(j p) d -> p j d", p=128), wc1[m], 8 * 128))
        conv_jobs.append((din["w2t"][:, m * 128:(m + 1) * 128]
                          .rearrange("(j p) d -> p j d", p=128), wc2[m], 8 * 128))
    for m in range(32):
        for half in range(2):
            conv_jobs.append((din["h1t"][half * 2048:(half + 1) * 2048, m * 128:(m + 1) * 128]
                              .rearrange("(j p) d -> p j d", p=128),
                              wch1[m][:, half * 2048:(half + 1) * 2048], 16 * 128))
            conv_jobs.append((din["h2t"][half * 2048:(half + 1) * 2048, m * 128:(m + 1) * 128]
                              .rearrange("(j p) d -> p j d", p=128),
                              wch2[m][:, half * 2048:(half + 1) * 2048], 16 * 128))
    for j in range(32):
        conv_jobs.append((din["w3t"][j * 128:(j + 1) * 128, :],
                          wc3[j * 128:(j + 1) * 128, :], D))
    cstate = {"i": 0, "done": []}

    def emit_conv(n):
        for _ in range(n):
            i = cstate["i"]
            if i >= len(conv_jobs):
                return
            src, dst, ncol = conv_jobs[i]
            buf = conv[i % 2][:, 0:ncol]
            if len(src.shape) == 3:
                buf = buf.rearrange("p (j d) -> p j d", d=128)
            w = [(qG, cstate["done"][i - 2])] if i >= 2 else []
            P.step("gpsimd", lambda e, b=buf, s=src: e.dma_start(out=b, in_=s),
                   waits=w, incs=[(qG, 16)])
            ldv = qG.v
            P.step("gpsimd", lambda e, b=buf, d=dst: e.dma_start(out=d, in_=b),
                   waits=[(qG, ldv)], incs=[(qG, 16)])
            cstate["done"].append(qG.v)
            cstate["i"] += 1

    def emit_ln(x_ap, sq_ap, c, sq_waits, stats_waits=()):
        P.step("scalar", lambda e: e.activation(out=sq_ap, in_=x_ap, func=AF.Square),
               waits=sq_waits, incs=[(se, 1)])
        se_sq = se.v

        def reds(e):
            e.reduce_sum(out=c[0], in_=x_ap, axis=AX.X)
            return e.reduce_sum(out=c[1], in_=sq_ap, axis=AX.X)
        P.step("vector", reds, waits=list(stats_waits) + [(se, se_sq)], incs=[(dv, 1)])
        dv_stats = dv.v
        P.step("vector", lambda e: e.scalar_tensor_tensor(
            out=c[2], in0=c[0], scalar=1.0 / (D * D), in1=c[0], op0=ALU.mult, op1=ALU.mult),
            waits=[(dv, dv_stats)], incs=[(dv, 1)])
        P.step("vector", lambda e: e.scalar_tensor_tensor(
            out=c[1], in0=c[1], scalar=1.0 / D, in1=c[2], op0=ALU.mult, op1=ALU.subtract),
            waits=[(dv, dv.v)], incs=[(dv, 1)])
        P.step("scalar", lambda e: e.activation(out=c[2], in_=c[1], func=AF.Sqrt, bias=epsB[:]),
               waits=[(dv, dv.v)], incs=[(se, 1)])
        P.step("vector", lambda e: e.reciprocal(out=c[2], in_=c[2]),
               waits=[(se, se.v)], incs=[(dv, 1)])
        P.step("vector", lambda e: e.scalar_tensor_tensor(
            out=c[3], in0=c[0], scalar=-1.0 / D, in1=c[2], op0=ALU.mult, op1=ALU.mult),
            waits=[(dv, dv.v)], incs=[(dv, 1)])
        return dv.v, dv_stats

    ln_jobs = []
    ln_hist = {}
    qa_hist = {}
    lnc = [tuple(st_sm[:, k + 4 * par:k + 4 * par + 1] for k in range(4))
           for par in range(4)]

    def ln_stage_load_A(i):
        src, dst = ln_jobs[i]
        par = i % 4
        lw = [(se, ln_hist[i - 4]["E"])] if i >= 4 else []
        P.step("sync", lambda e, d=xt[par][:], s=src: e.dma_start(out=d, in_=s),
               waits=lw, incs=[(qS, 16)])
        qsv = qS.v
        c = lnc[par]
        aw = [(qS, qsv)]
        if i >= 4:
            aw.append((qA, qa_hist[i - 4]))
        P.step("scalar", lambda e, par=par, c=c: e.activation(
            out=xnt[par][:], in_=xt[par][:], func=AF.Square, accum_out=c[1]),
            waits=aw, incs=[(se, 1)])
        ln_hist[i] = {"A": se.v}

    def ln_stage_BCD(i):
        par = i % 4
        c = lnc[par]
        h = ln_hist[i]
        P.step("vector", lambda e, par=par, c=c: e.reduce_sum(
            out=c[0], in_=xt[par][:], axis=AX.X), waits=[(se, h["A"])], incs=[(dv, 1)])
        P.step("vector", lambda e, c=c: e.scalar_tensor_tensor(
            out=c[2], in0=c[0], scalar=1.0 / (D * D), in1=c[0], op0=ALU.mult, op1=ALU.mult),
            waits=[(dv, dv.v)], incs=[(dv, 1)])
        P.step("vector", lambda e, c=c: e.scalar_tensor_tensor(
            out=c[1], in0=c[1], scalar=1.0 / D, in1=c[2], op0=ALU.mult, op1=ALU.subtract),
            waits=[(dv, dv.v)], incs=[(dv, 1)])
        P.step("scalar", lambda e, c=c: e.activation(
            out=c[2], in_=c[1], func=AF.Sqrt, bias=epsB[:]),
            waits=[(dv, dv.v)], incs=[(se, 1)])
        P.step("vector", lambda e, c=c: e.reciprocal(out=c[2], in_=c[2]),
               waits=[(se, se.v)], incs=[(dv, 1)])
        P.step("vector", lambda e, c=c: e.scalar_tensor_tensor(
            out=c[3], in0=c[0], scalar=-1.0 / D, in1=c[2], op0=ALU.mult, op1=ALU.mult),
            waits=[(dv, dv.v)], incs=[(dv, 1)])
        h["D"] = dv.v

    def ln_stage_E(i):
        src, dst = ln_jobs[i]
        par = i % 4
        c = lnc[par]
        h = ln_hist[i]
        nw = [(dv, h["D"])]
        P.step("scalar", lambda e, par=par, c=c: e.activation(
            out=xnt[par][:], in_=xt[par][:], func=AF.Identity, bias=c[3], scale=c[2]),
            waits=nw, incs=[(se, 1)])
        h["E"] = se.v
        P.step("scalar", lambda e, par=par, d=dst: e.dma_start(out=d, in_=xnt[par][:]),
               waits=[(se, h["E"])], incs=[(qA, 16)])
        qa_hist[i] = qA.v

    pstate = [dict() for _ in range(NPAIR)]
    prev_pair = {"exp_se": 0, "av_pe": 0, "ot_dv": dv_s3b, "nm_pe": 0}
    r4dv = {}
    QGI = [0]
    prep = []

    def drain_prep(n):
        for _ in range(n):
            if not prep:
                return
            prep.pop(0)()
            emit_conv(1)

    def emit_pair_weights(p, wlw):
        wpar = p % 2
        first = [True]
        for wsrc, wdst in ((din["wk"], wkb[wpar]), (din["wv"], wvb[wpar]),
                           (din["wq"], wqb[wpar])):
            for j in range(8):
                P.step("gpsimd", lambda e, s=wsrc[j * 128:(j + 1) * 128, p * 128:(p + 1) * 128],
                       d=wdst[:, j * 128:(j + 1) * 128]: e.dma_start(out=d, in_=s),
                       waits=wlw if first[0] else [], incs=[(qG, 16)])
                first[0] = False
        pstate[p]["qg_w"] = qG.v

    def _hc_buf(j):
        return xnstr[j // 4][:, (j % 4) * 512:(j % 4 + 1) * 512]

    def fill_prep(p, ln_gate=False):
        st = pstate[p]
        st["kmm"] = []
        st["evac_se"] = []
        st["evac_dv"] = []
        kb = K2[p % 2]
        guard = pstate[p - 1]["last_qpe"] if p >= 1 else None
        for hh in range(16):
            def loads(hh=hh, st=st, p=p, guard=guard):
                for j in range(8):
                    lw = []
                    if j == 0:
                        if st["kmm"]:
                            lw.append((pe, st["kmm"][-1]))
                        elif guard is not None:
                            lw.append((pe, guard))
                        if ln_gate:
                            t = 4 * hh + 3
                            lw.append((qA, qa_hist[10 * (t // 8) + t % 8]))
                    P.step("sync", lambda e, b=_hc_buf(j), hh=hh, j=j: e.dma_start(
                        out=b, in_=xnf[hh * 512:(hh + 1) * 512, j * 128:(j + 1) * 128],
                        transpose=True), waits=lw, incs=[(qS, 16)])
                st["qs_hc"] = qS.v

            def mats(hh=hh, st=st, p=p, kb=kb):
                wpar = p % 2

                def mk(e, wpar=wpar):
                    last = None
                    for j in range(8):
                        buf = _hc_buf(j)
                        st_, sp = j == 0, j == 7
                        e.matmul(psB[:, 1024:1536], wkb[wpar][:, j * 128:(j + 1) * 128],
                                 buf, start=st_, stop=sp, skip_group_check=True)
                        last = e.matmul(psB[:, 1536:2048], wvb[wpar][:, j * 128:(j + 1) * 128],
                                        buf, start=st_, stop=sp, skip_group_check=True)
                    return last
                w = [(qS, st["qs_hc"]), (qG, st["qg_w"])]
                if hh == 0:
                    if p > 0:
                        w.append((dv, pstate[p - 1]["dv_nm"]))
                else:
                    w += [(se, st["evac_se"][hh - 1]), (dv, st["evac_dv"][hh - 1])]
                P.step("tensor", mk, waits=w, incs=[(pe, 1)])
                st["kmm"].append(pe.v)
                P.step("scalar", lambda e, hh=hh, kb=kb: e.copy(
                    out=kb[:, hh * 512:(hh + 1) * 512], in_=psB[:, 1024:1536]),
                    waits=[(pe, pe.v)], incs=[(se, 1)])
                st["evac_se"].append(se.v)
                P.step("vector", lambda e, hh=hh: e.tensor_copy(
                    out=V2[:, hh * 512:(hh + 1) * 512], in_=psB[:, 1536:2048]),
                    waits=[(pe, st["kmm"][hh])], incs=[(dv, 1)])
                st["evac_dv"].append(dv.v)
            prep.append(loads)
            prep.append(mats)

    def emit_p2c(p, qa_wait=None):
        st = pstate[p]
        wpar = p % 2
        qbuf_pe = []
        for j in range(8):
            buf = xnstr[j % 2]
            lw = [(pe, st["kmm"][-1])] if j < 2 else [(pe, qbuf_pe[j - 2])]
            if qa_wait is not None and j == 0:
                lw.append((qA, qa_wait))
            P.step("sync", lambda e, b=buf[:, 0:2048], j=j:
                   e.dma_start(out=b, in_=xnod[:, j * 128:(j + 1) * 128], transpose=True),
                   waits=lw, incs=[(qS, 16)])
            qsv = qS.v

            def mk_q(e, j=j, buf=buf, wpar=wpar):
                st_, sp = j == 0, j == 7
                last = None
                for cq in range(4):
                    last = e.matmul(psA[:, cq * 512:(cq + 1) * 512],
                                    wqb[wpar][:, j * 128:(j + 1) * 128],
                                    buf[:, cq * 512:(cq + 1) * 512],
                                    start=st_, stop=sp, skip_group_check=True)
                return last
            w = [(qS, qsv)]
            if j == 0:
                w += [(qG, st["qg_w"]), (se, st["evac_se"][15]), (dv, st["evac_dv"][15])]
                if p > 0:
                    w.append((se, prev_pair["exp_se"]))
                else:
                    w.append((dv, dv_s3b))
            P.step("tensor", mk_q, waits=w, incs=[(pe, 1)])
            qbuf_pe.append(pe.v)
        st["last_qpe"] = pe.v
        qw = [(pe, pe.v)]
        if p > 0:
            qw.append((pe, prev_pair["av_pe"]))
        P.step("scalar", lambda e: e.activation(out=Q2[:, 0:1024], in_=psA[:, 0:1024],
                                                func=AF.Copy, scale=0.125),
               waits=qw, incs=[(se, 1)])
        P.step("scalar", lambda e: e.activation(out=Q2[:, 1024:2048], in_=psA[:, 1024:2048],
                                                func=AF.Copy, scale=0.125),
               incs=[(se, 1)])
        st["se_q2"] = se.v

    def emit_p2b(p):
        st = pstate[p]
        se_k2 = st["evac_se"][15]
        dv_v2 = st["evac_dv"][15]
        vtm_dv = []
        for g in range(16):
            gp = g % 2

            def mk_tp(e, g=g, gp=gp):
                last = None
                for jj in range(4):
                    j = g * 4 + jj
                    for h in range(2):
                        last = e.matmul(
                            psB[:, (gp * 2 + h) * 512 + jj * 64:
                                (gp * 2 + h) * 512 + (jj + 1) * 64],
                            V2[h * 64:(h + 1) * 64, j * 128:(j + 1) * 128],
                            ident[h * 64:(h + 1) * 64, h * 64:(h + 1) * 64],
                            start=True, stop=True, skip_group_check=True)
                return last
            if g >= 2:
                w = [(dv, vtm_dv[g - 2])]
            else:
                w = [(dv, dv_v2), (se, se_k2)]
                if p > 0:
                    w += [(pe, prev_pair["av_pe"]), (dv, prev_pair["ot_dv"])]
            P.step("tensor", mk_tp, waits=w, incs=[(pe, 1)])
            pe_tp = pe.v
            vw = [(pe, pe_tp)]
            if g == 0:
                vw.append((dv, dv_vtones))
                if p > 0:
                    vw.append((pe, prev_pair["av_pe"]))

            def mk_vcp(e, g=g, gp=gp):
                vt3 = Vtm.ap().rearrange("p (b c) -> p b c", c=65)
                last = None
                for h in range(2):
                    d = vt3[:, g * 8 + h:(g + 1) * 8:2, 0:64]
                    s = psB[:, (gp * 2 + h) * 512:(gp * 2 + h) * 512 + 256] \
                        .rearrange("p (b c) -> p b c", c=64)
                    last = e.tensor_copy(out=d, in_=s)
                return last
            P.step("vector", mk_vcp, waits=vw, incs=[(dv, 1)])
            vtm_dv.append(dv.v)
        st["dv_vtm"] = dv.v

    def emit_p2d(p):
        st = pstate[p]
        se_k2 = st["evac_se"][15]
        dv_vtm = st["dv_vtm"]
        kb = K2[p % 2]
        P.step("scalar", lambda e, kb=kb: e.activation(out=V2[:, 0:S], in_=kb[:], func=AF.Square),
               waits=[(se, se_k2), (dv, dv_vtm)], incs=[(se, 1)])
        se_ksq = se.v
        kn_dv = []
        for r in range(4):
            def mk_kn(e, r=r):
                last = None
                for cc in range(4):
                    last = e.matmul(psB[0:1, cc * 512:(cc + 1) * 512], ones_c1[:],
                                    V2[:, r * 2048 + cc * 512:r * 2048 + (cc + 1) * 512],
                                    start=True, stop=True, skip_group_check=True)
                return last
            w = [(se, se_ksq)]
            w.append((dv, dv_vtm if r == 0 else kn_dv[r - 1]))
            P.step("tensor", mk_kn, waits=w, incs=[(pe, 1)])

            P.step("vector", lambda e: e.reduce_max(
                out=sm1[0:1, 1:2], in_=psB[0:1, 0:2048], axis=AX.X),
                waits=[(pe, pe.v)], incs=[(dv, 1)])
            if r == 0:
                P.step("vector", lambda e: e.tensor_copy(out=sm1[0:1, 0:1], in_=sm1[0:1, 1:2]),
                       waits=[(dv, dv.v)], incs=[(dv, 1)])
            else:
                P.step("vector", lambda e: e.tensor_tensor(
                    out=sm1[0:1, 0:1], in0=sm1[0:1, 0:1], in1=sm1[0:1, 1:2], op=ALU.max),
                    waits=[(dv, dv.v)], incs=[(dv, 1)])
            kn_dv.append(dv.v)
        P.step("scalar", lambda e: e.activation(out=V2[:, 0:TOWN], in_=Q2[:], func=AF.Square),
               waits=[(se, st["se_q2"]), (dv, kn_dv[-1])], incs=[(se, 1)])
        se_qsq = se.v

        def mk_qn(e):
            last = None
            for cc in range(4):
                last = e.matmul(psB[0:1, cc * 512:(cc + 1) * 512], ones_c1[:],
                                V2[:, cc * 512:(cc + 1) * 512],
                                start=True, stop=True, skip_group_check=True)
            return last
        P.step("tensor", mk_qn, waits=[(se, se_qsq), (dv, kn_dv[-1])], incs=[(pe, 1)])

        P.step("vector", lambda e: e.reduce_max(
            out=sm1[0:1, 2:3], in_=psB[0:1, 0:2048], axis=AX.X),
            waits=[(pe, pe.v)], incs=[(dv, 1)])
        P.step("vector", lambda e: e.tensor_tensor(
            out=sm1[0:1, 3:4], in0=sm1[0:1, 2:3], in1=sm1[0:1, 0:1], op=ALU.mult),
            waits=[(dv, dv.v)], incs=[(dv, 1)])
        dv_m2 = dv.v
        P.step("scalar", lambda e: e.activation(out=sm1[0:1, 4:5], in_=sm1[0:1, 3:4], func=AF.Sqrt),
               waits=[(dv, dv_m2)], incs=[(se, 1)])
        P.step("scalar", lambda e: e.activation(out=sm1[0:1, 5:6], in_=sm1[0:1, 4:5],
                                                func=AF.Copy, scale=-1.02, bias=-0.5),
               waits=[(se, se.v)], incs=[(se, 1)])
        nmw = [(se, se.v)]
        if p > 0:
            nmw.append((pe, prev_pair["nm_pe"]))
        P.step("scalar", lambda e: e.activation(out=nmb16[:], in_=sm1[0:1, 5:6], func=AF.Copy),
               waits=nmw, incs=[(se, 1)])
        se_nm = se.v
        P.step("tensor", lambda e: e.matmul(psB[0:128, 0:1], ones_r128[:], nmb16[:],
                                            start=True, stop=True, skip_group_check=True),
               waits=[(se, se_nm), (dv, dv_m2)], incs=[(pe, 1)])
        prev_pair["nm_pe"] = pe.v
        nmcw = [(pe, pe.v)]
        if p > 0:
            nmcw.append((se, prev_pair["exp_se"]))
        P.step("vector", lambda e: e.tensor_copy(out=nmB[:], in_=psB[0:128, 0:1]),
               waits=nmcw, incs=[(dv, 1)])
        st["dv_nm"] = dv.v

    def emit_p2f(p):
        st = pstate[p]
        dv_nm = st["dv_nm"]
        dv_vtm = st["dv_vtm"]
        se_q2 = st["se_q2"]
        kb = K2[p % 2]
        slot_exp_se = []
        slot_av_pe = []
        slot_mm_pe = []
        diag_dv = []
        pend_av = None
        p2g_pend = [None]
        s_idx = 0

        def p2g_head(q, pe_q_last):
            qg = QGI[0]
            stt = {}
            for h in (0, 1):
                P.step("scalar", lambda e, h=h: e.activation(
                    out=rinv[0:1, h * 512:(h + 1) * 512],
                    in_=psB[64:65, h * 512:h * 512 + 512], func=AF.Ln),
                    waits=[(pe, pe_q_last)] if h == 0 else [(se, se.v)], incs=[(se, 1)])
                P.step("scalar", lambda e, h=h: e.activation(
                    out=rinv[0:1, h * 512:(h + 1) * 512],
                    in_=rinv[0:1, h * 512:(h + 1) * 512], func=AF.Exp, scale=-1.0),
                    waits=[(se, se.v)], incs=[(se, 1)])
                stt[h] = se.v
            return (qg, q, stt)

        def p2g_tail(info):
            qg, q, stt = info
            for h in (0, 1):
                P.step("tensor", lambda e, h=h: e.matmul(
                    psB[64:128, h * 512:h * 512 + 512], ones_r64f[:],
                    rinv[0:1, h * 512:(h + 1) * 512],
                    start=True, stop=True, skip_group_check=True),
                    waits=[(se, stt[h])], incs=[(pe, 1)])
                w = [(pe, pe.v)]
                if (qg - 1, h) in r4dv:
                    w.append((dv, r4dv[(qg - 1, h)]))
                P.step("scalar", lambda e, h=h: e.copy(
                    out=rB[:, h * 512:(h + 1) * 512],
                    in_=psB[64:128, h * 512:h * 512 + 512]),
                    waits=w, incs=[(se, 1)])
                sev = se.v
                P.step("vector", lambda e, h=h, q=q: e.tensor_tensor(
                    out=otbuf[h * 64:(h + 1) * 64,
                              p * TOWN + q * 512:p * TOWN + (q + 1) * 512],
                    in0=psB[0:64, h * 512:h * 512 + 512],
                    in1=rB[:, h * 512:(h + 1) * 512], op=ALU.mult),
                    waits=[(se, sev)], incs=[(dv, 1)])
                r4dv[(qg, h)] = dv.v

        for q in range(4):
            qg = QGI[0]
            qlo, qhi = 4 * q, 4 * q + 3
            jmax = 4 * (qhi + 1)
            for j in range(jmax):
                par_s = s_idx % 2
                blo = max(j // 4, qlo)
                pst = psA[:, par_s * 1024:(par_s + 1) * 1024]
                lo = (blo - qlo) * 128
                hi = (qhi - qlo + 1) * 128

                def mk_st(e, j=j, blo=blo, qhi=qhi, pst=pst, lo=lo, hi=hi, kb=kb):
                    last = None
                    for h in range(2):
                        last = e.matmul(
                            pst[:, h * 512 + lo:h * 512 + hi],
                            kb[h * 64:(h + 1) * 64, j * 128:(j + 1) * 128],
                            Q2[h * 64:(h + 1) * 64, blo * 128:(qhi + 1) * 128],
                            start=True, stop=True, skip_group_check=True)
                    return last
                w = []
                if s_idx >= 2:
                    w.append((se, slot_exp_se[s_idx - 2]))
                else:
                    w += [(se, se_q2), (dv, dv_nm)]
                P.step("tensor", mk_st, waits=w, incs=[(pe, 1)])
                slot_mm_pe.append(pe.v)

                if j == 1 and p2g_pend[0] is not None:
                    p2g_tail(p2g_pend[0])
                    p2g_pend[0] = None

                if pend_av is not None:
                    aw = list(pend_av[1])
                    if pend_av[2] is not None:
                        aw.append((dv, r4dv[pend_av[2]]))
                    P.step("tensor", pend_av[0], waits=aw, incs=[(pe, 1)])
                    slot_av_pe.append(pe.v)
                    pend_av = None
                    if s_idx % 4 == 2:
                        drain_prep(1)

                ew = [(pe, slot_mm_pe[s_idx]), (dv, dv_nm)]
                if s_idx >= 2:
                    ew.append((pe, slot_av_pe[s_idx - 2]))
                P.step("scalar", lambda e, pst=pst, par_s=par_s: e.activation(
                    out=PT[par_s][:], in_=pst, func=AF.Exp, bias=nmB[:]),
                    waits=ew, incs=[(se, 1)])
                slot_exp_se.append(se.v)

                if qlo <= j // 4 <= qhi:
                    b = j // 4
                    srel = j - 4 * b
                    off = (b - qlo) * 128

                    def mk_msk(e, b=b, srel=srel, off=off, par_s=par_s):
                        last = None
                        for h in range(2):
                            sl = PT[par_s][:, h * 512 + off:h * 512 + off + 128]
                            last = e.tensor_tensor(
                                out=sl, in0=sl,
                                in1=maskT[:, (b * 4 + srel) * 128:(b * 4 + srel + 1) * 128],
                                op=ALU.mult)
                        return last
                    P.step("vector", mk_msk, waits=[(se, slot_exp_se[s_idx])], incs=[(dv, 1)])
                    diag_dv.append(dv.v)
                else:
                    diag_dv.append(None)

                def mk_av(e, j=j, jmax=jmax, par_s=par_s, lo=lo, hi=hi):
                    last = None
                    for h in range(2):
                        last = e.matmul(
                            psB[0:65, h * 512 + lo:h * 512 + hi],
                            Vtm[:, (j * 2 + h) * 65:(j * 2 + h) * 65 + 65],
                            PT[par_s][:, h * 512 + lo:h * 512 + hi],
                            start=(j == 0), stop=(j == jmax - 1), skip_group_check=True)
                    return last
                avw = [(se, slot_exp_se[s_idx])]
                reg_free = None
                if diag_dv[s_idx] is not None:
                    avw.append((dv, diag_dv[s_idx]))
                if j == 0:
                    if q == 0:
                        avw.append((dv, dv_nm))
                    else:
                        reg_free = (qg - 1, 1)
                    avw.append((dv, dv_vtm))
                pend_av = (mk_av, avw, reg_free)
                s_idx += 1
            P.step("tensor", pend_av[0], waits=pend_av[1], incs=[(pe, 1)])
            slot_av_pe.append(pe.v)
            pend_av = None
            p2g_pend[0] = p2g_head(q, pe.v)
            QGI[0] += 1
        p2g_tail(p2g_pend[0])
        p2g_pend[0] = None
        prev_pair.update({
            "exp_se": slot_exp_se[-1],
            "av_pe": slot_av_pe[-1],
            "ot_dv": r4dv[(QGI[0] - 1, 1)],
        })

    emit_pair_weights(0, [])
    emit_pair_weights(1, [])
    fill_prep(0, ln_gate=True)

    for T in range(8):
        for k in range(8):
            i = 8 * T + k
            ln_jobs.append((din["xk"][i * 128:(i + 1) * 128, :],
                            xnf[i * 128:(i + 1) * 128, :]))
        for k in range(2):
            i = 2 * T + k
            ln_jobs.append((din["xo"][i * 128:(i + 1) * 128, :],
                            xnod[i * 128:(i + 1) * 128, :]))
    NJ = len(ln_jobs)
    chunks_done = [0]

    def maybe_chunk(i_store):
        while chunks_done[0] < 8:
            T = chunks_done[0]
            trigger = 10 * (T + 1) + 7 if T < 7 else NJ - 1
            if i_store != trigger:
                break
            emit_conv(2)
            drain_prep(4)
            chunks_done[0] += 1
            if T < 7:
                break
    for i in range(NJ + 2):
        if i < NJ:
            ln_stage_load_A(i)
        if 1 <= i <= NJ:
            ln_stage_BCD(i - 1)
        if i >= 2:
            ln_stage_E(i - 2)
            maybe_chunk(i - 2)
    drain_prep(len(prep))
    emit_p2c(0, qa_wait=qa_hist[NJ - 1])
    emit_p2b(0)
    emit_p2d(0)
    for p in range(NPAIR):
        if p + 1 < NPAIR:
            if p >= 1:
                emit_pair_weights(p + 1, [(pe, pstate[p - 1]["last_qpe"])])
            fill_prep(p + 1)
        emit_p2f(p)
        if p + 1 < NPAIR:
            drain_prep(len(prep))
            emit_p2c(p + 1)
            emit_p2b(p + 1)
            emit_p2d(p + 1)

    emit_conv(len(conv_jobs))
    qg_conv_all = qG.v
    dv_ot_final = prev_pair["ot_dv"]
    if stop_after == "P2":
        return _early_exit("P2")

    p2s.close()
    wob = sb(pOTs, "wob", [128, 8 * 1024], BF16)
    y1_sb = [sb(pOTs, f"y1sb{i}", [128, D], F32) for i in range(2)]
    for pch in range(8):
        P.step("gpsimd", lambda e, pch=pch: e.dma_start(
            out=wob[:, pch * 1024:(pch + 1) * 1024],
            in_=din["wo"][pch * 128:(pch + 1) * 128, :]),
            waits=[(qG, qg_conv_all), (dv, dv_ot_final)] if pch == 0 else [],
            incs=[(qG, 16)])
    qg_wob = qG.v

    h_setf = []
    h_dvstats = []
    h_dvy1 = []
    h_qaxn2 = []
    h_sexn2 = []
    h_dvy1fin = []
    h_qgy1 = []
    for i in range(16):
        par = i % 2
        reg = psA[:, par * 1024:(par + 1) * 1024]
        lw = [(dv, h_dvy1[i - 2])] if i >= 2 else [(dv, dv_ot_final)]
        P.step("sync", lambda e, i=i, par=par: e.dma_start(
            out=xt[par][:], in_=din["xo"][i * 128:(i + 1) * 128, :]),
            waits=lw, incs=[(qS, 16)])
        qs_xo = qS.v

        def mk_wo(e, i=i, reg=reg):
            last = None
            for pch in range(8):
                st, sp = pch == 0, pch == 7
                lhsT = otbuf[:, pch * TOWN + i * 128:pch * TOWN + (i + 1) * 128]
                e.matmul(reg[:, 0:512], lhsT, wob[:, pch * 1024:pch * 1024 + 512],
                         start=st, stop=sp, skip_group_check=True)
                last = e.matmul(reg[:, 512:1024], lhsT, wob[:, pch * 1024 + 512:(pch + 1) * 1024],
                                start=st, stop=sp, skip_group_check=True)
            return last
        w = [(qG, qg_wob), (dv, dv_ot_final)]
        if i >= 2:
            w += [(se, h_setf[i - 2]), (dv, h_dvstats[i - 2])]
        P.step("tensor", mk_wo, waits=w, incs=[(pe, 1)])
        pe_wo = pe.v

        c = tuple(st_sm[:, k + 8 * par:k + 8 * par + 1] for k in range(4))
        dv_fin, dv_stats = emit_ln(reg, sqt[par][:], c,
                                   sq_waits=[(pe, pe_wo)], stats_waits=[(pe, pe_wo)])
        tw = [(dv, dv_fin)]
        if i >= 1:
            tw.append((dv, h_dvy1fin[i - 1]))
        P.step("scalar", lambda e, reg=reg, c=c: e.activation(
            out=t_sb[:], in_=reg, func=AF.Identity, bias=c[3], scale=c[2]),
            waits=tw, incs=[(se, 1)])
        h_setf.append(se.v)
        h_dvstats.append(dv_stats)
        yw = [(se, se.v), (qS, qs_xo)]
        if i >= 2:
            yw.append((qG, h_qgy1[i - 2]))
        P.step("vector", lambda e, par=par: e.tensor_add(
            out=y1_sb[par][:], in0=t_sb[:], in1=xt[par][:]),
            waits=yw, incs=[(dv, 1)])
        h_dvy1.append(dv.v)
        h_dvy1fin.append(dv.v)
        P.step("gpsimd", lambda e, i=i, par=par: e.dma_start(
            out=y1d[i * 128:(i + 1) * 128, :], in_=y1_sb[par][:]),
            waits=[(dv, dv.v)], incs=[(qG, 16)])
        h_qgy1.append(qG.v)

        dv_fin2, _ = emit_ln(y1_sb[par][:], sqt[par][:], c,
                             sq_waits=[(dv, h_dvy1[i])], stats_waits=[])
        nw = [(dv, dv_fin2)]
        if i >= 2:
            nw.append((qA, h_qaxn2[i - 2]))
        P.step("scalar", lambda e, par=par, c=c: e.activation(
            out=xnt[par][:], in_=y1_sb[par][:], func=AF.Identity, bias=c[3], scale=c[2]),
            waits=nw, incs=[(se, 1)])
        h_sexn2.append(se.v)
        P.step("scalar", lambda e, i=i, par=par: e.dma_start(
            out=xn2d[i * 128:(i + 1) * 128, :], in_=xnt[par][:]),
            waits=[(se, h_sexn2[-1])], incs=[(qA, 16)])
        h_qaxn2.append(qA.v)
    qa_p3a = qA.v
    qg_p3a = qG.v
    if stop_after == "P3a":
        return _early_exit("P3a")
    se_p3a = se.v
    dv_p3a = dv.v

    if debug:
        P.step("gpsimd", lambda e: e.dma_start(out=dbg["y1"], in_=y1d),
               waits=[(qG, qg_p3a), (qA, qa_p3a)], incs=[(qG, 16)])
        P.step("gpsimd", lambda e: e.dma_start(out=dbg["xn2"], in_=xn2d),
               waits=[(qG, qg_p3a), (qA, qa_p3a)], incs=[(qG, 16)])

    pOTs.close()
    out_sb = [sb(es, f"outsb{i}", [128, D], F32) for i in range(2)]
    A_buf = sb(es, "A_buf", [128, 32 * 512], BF16)
    B_buf = sb(es, "B_buf", [128, 32 * 512], BF16)
    xn2f = sb(es, "xn2f", [128, 8 * 512], BF16)
    w12s = [sb(es, f"w12s{i}", [128, 2 * 8 * 128], BF16) for i in range(4)]
    hstr = [sb(es, f"hstr{i}", [128, 32 * 128], BF16) for i in range(2)]
    w3s = [sb(es, f"w3s{i}", [128, 1024], BF16) for i in range(4)]
    hA = [sb(es, f"hA{i}", [128, 512], F32) for i in range(2)]
    hG = [sb(es, f"hG{i}", [128, 512], F32) for i in range(2)]
    f_sb = [sb(es, f"fsb{i}", [128, D], F32) for i in range(2)]

    n_out = 0
    prev_f_dv = dv_p3a
    prev_w1_pe = pe.v
    h_dvoutfin = []
    h_qout = []
    h_se_t3 = []
    gidx = 0
    for T in range(4):
        t0 = T * 512
        for j in range(8):
            w = []
            if T == 0 and j == 0:
                w.append((qA, qa_p3a))
            if T > 0 and j == 0:
                w.append((pe, prev_w1_pe))
            P.step("sync", lambda e, j=j, t0=t0: e.dma_start(
                out=xn2f[:, j * 512:(j + 1) * 512],
                in_=xn2d[t0:t0 + 512, j * 128:(j + 1) * 128], transpose=True),
                waits=w, incs=[(qS, 16)])
        qs_xn2f = qS.v

        w12_pe = {}
        w12_se = {}
        w12_dv = {}
        for m in range(32):
            par = m % 4
            par2 = m % 2
            lw = [(pe, w12_pe[m - 4])] if m >= 4 else []
            P.step("sync", lambda e, m=m, par=par: e.dma_start(
                out=w12s[par][:, 0:1024], in_=wc1[m]), waits=lw, incs=[(qS, 16)])
            P.step("sync", lambda e, m=m, par=par: e.dma_start(
                out=w12s[par][:, 1024:2048], in_=wc2[m]), incs=[(qS, 16)])
            qs_w = qS.v
            bank = psA if par < 2 else psB
            ps1 = bank[:, (par % 2) * 512:(par % 2) * 512 + 512]
            ps2 = bank[:, 1024 + (par % 2) * 512:1024 + (par % 2) * 512 + 512]

            def mk12(e, par=par, ps1=ps1, ps2=ps2):
                last = None
                for j in range(8):
                    st, sp = j == 0, j == 7
                    rhs = xn2f[:, j * 512:(j + 1) * 512]
                    e.matmul(ps1, w12s[par][:, j * 128:(j + 1) * 128], rhs,
                             start=st, stop=sp, skip_group_check=True)
                    last = e.matmul(ps2, w12s[par][:, 1024 + j * 128:1024 + (j + 1) * 128], rhs,
                                    start=st, stop=sp, skip_group_check=True)
                return last
            w = [(qS, qs_w)]
            if m >= 4:
                w += [(se, w12_se[m - 4]), (dv, w12_dv[m - 4])]
            else:
                w.append((dv, prev_f_dv))
                if T == 0:
                    w += [(se, se_p3a), (dv, dv_p3a), (qG, qg_p3a)]
            P.step("tensor", mk12, waits=w, incs=[(pe, 1)])
            w12_pe[m] = pe.v
            P.step("scalar", lambda e, ps1=ps1, m=m, par2=par2: e.activation(
                out=hA[par2][:], in_=ps1, func=AF.Silu, scale=scl_s1[:, m:m + 1]),
                waits=[(pe, pe.v)] + ([(dv, w12_dv[m - 2])] if m >= 2 else []),
                incs=[(se, 1)])
            w12_se[m] = se.v
            P.step("vector", lambda e, ps2=ps2, m=m, par2=par2: e.tensor_scalar(
                out=hG[par2][:], in0=ps2, scalar1=scl_s2[:, m:m + 1], scalar2=None, op0=ALU.mult),
                waits=[(pe, w12_pe[m])], incs=[(dv, 1)])
            P.step("vector", lambda e, m=m, par2=par2: e.tensor_tensor(
                out=A_buf[:, m * 512:(m + 1) * 512], in0=hA[par2][:], in1=hG[par2][:],
                op=ALU.mult),
                waits=[(se, w12_se[m])], incs=[(dv, 1)])
            w12_dv[m] = dv.v
        dv_A = dv.v
        prev_w1_pe = w12_pe[31]
        if debug and T == 0:
            P.step("gpsimd", lambda e: e.dma_start(out=dbg["A0"], in_=A_buf[:]),
                   waits=[(dv, dv_A)], incs=[(qG, 16)])
            dv_A = dv.v
            h1_dbg_qg = qG.v

        h1_pe = {}
        h1_se = {}
        for m in range(32):
            par = m % 4
            lw = [(pe, h1_pe[m - 2])] if m >= 2 else [(pe, w12_pe[30 + m])]
            P.step("sync", lambda e, m=m, par=par % 2: e.dma_start(
                out=hstr[par][:], in_=wch1[m]), waits=lw, incs=[(qS, 16)])
            qs_h = qS.v
            bank = psA if par < 2 else psB
            ps1 = bank[:, (par % 2) * 512:(par % 2) * 512 + 512]

            def mkh1(e, par=par % 2, ps1=ps1):
                last = None
                for j in range(32):
                    last = e.matmul(ps1, hstr[par][:, j * 128:(j + 1) * 128],
                                    A_buf[:, j * 512:(j + 1) * 512],
                                    start=(j == 0), stop=(j == 31), skip_group_check=True)
                return last
            w = [(qS, qs_h), (dv, dv_A)]
            if m >= 4:
                w.append((se, h1_se[m - 4]))
            else:
                w.append((se, w12_se[28 + m]))
            P.step("tensor", mkh1, waits=w, incs=[(pe, 1)])
            h1_pe[m] = pe.v
            P.step("scalar", lambda e, ps1=ps1, m=m: e.activation(
                out=B_buf[:, m * 512:(m + 1) * 512], in_=ps1, func=AF.Gelu,
                scale=scl_h1[:, m:m + 1]),
                waits=[(pe, pe.v)], incs=[(se, 1)])
            h1_se[m] = se.v
        se_B = se.v
        if debug and T == 0:
            P.step("gpsimd", lambda e: e.dma_start(out=dbg["B0"], in_=B_buf[:]),
                   waits=[(se, se_B)], incs=[(qG, 16)])

        h2_pe = {}
        h2_se = {}
        for m in range(32):
            par = m % 4
            lw = [(pe, h2_pe[m - 2])] if m >= 2 else [(pe, h1_pe[30 + m])]
            P.step("sync", lambda e, m=m, par=par % 2: e.dma_start(
                out=hstr[par][:], in_=wch2[m]), waits=lw, incs=[(qS, 16)])
            qs_h = qS.v
            bank = psA if par < 2 else psB
            ps1 = bank[:, (par % 2) * 512:(par % 2) * 512 + 512]

            def mkh2(e, par=par % 2, ps1=ps1):
                last = None
                for j in range(32):
                    last = e.matmul(ps1, hstr[par][:, j * 128:(j + 1) * 128],
                                    B_buf[:, j * 512:(j + 1) * 512],
                                    start=(j == 0), stop=(j == 31), skip_group_check=True)
                return last
            w = [(qS, qs_h), (se, se_B)]
            if m >= 4:
                w.append((se, h2_se[m - 4]))
            else:
                w.append((se, h1_se[28 + m]))
            P.step("tensor", mkh2, waits=w, incs=[(pe, 1)])
            h2_pe[m] = pe.v
            h2w = [(pe, pe.v)]
            if debug and T == 0 and m < 2:
                h2w.append((qG, h1_dbg_qg))
            P.step("scalar", lambda e, ps1=ps1, m=m: e.activation(
                out=A_buf[:, m * 512:(m + 1) * 512], in_=ps1, func=AF.Gelu,
                scale=scl_h2[:, m:m + 1]),
                waits=h2w, incs=[(se, 1)])
            h2_se[m] = se.v
        se_A2 = se.v
        if debug and T == 0:
            P.step("gpsimd", lambda e: e.dma_start(out=dbg["A2"], in_=A_buf[:]),
                   waits=[(se, se_A2)], incs=[(qG, 16)])

        w3_pe = {}
        for j in range(32):
            par = j % 4
            lw = [(pe, w3_pe[j - 4])] if j >= 4 else []
            P.step("sync", lambda e, j=j, par=par: e.dma_start(
                out=w3s[par][:], in_=wc3[j * 128:(j + 1) * 128, :]),
                waits=lw, incs=[(qS, 16)])
            qs_w3 = qS.v

            def mkw3(e, j=j, par=par):
                last = None
                for sI in range(4):
                    ps = psA if sI < 2 else psB
                    regw = ps[:, (sI % 2) * 1024:(sI % 2) * 1024 + 1024]
                    lhsT = A_buf[:, j * 512 + sI * 128:j * 512 + (sI + 1) * 128]
                    e.matmul(regw[:, 0:512], lhsT, w3s[par][:, 0:512],
                             start=(j == 0), stop=(j == 31), skip_group_check=True)
                    last = e.matmul(regw[:, 512:1024], lhsT, w3s[par][:, 512:1024],
                                    start=(j == 0), stop=(j == 31), skip_group_check=True)
                return last
            w = [(qS, qs_w3), (se, se_A2)]
            if j == 0:
                w.append((dv, prev_f_dv))
                if T == 0:
                    w.append((dv, dv_p3a))
            P.step("tensor", mkw3, waits=w, incs=[(pe, 1)])
            w3_pe[j] = pe.v
        pe_w3 = pe.v

        for sI in range(4):
            par = sI % 2
            ps = psA if sI < 2 else psB
            regw = ps[:, (sI % 2) * 1024:(sI % 2) * 1024 + 1024]
            fw = [(pe, pe_w3), (dv, dv_s3b)]
            if gidx >= 2:
                fw.append((se, h_se_t3[gidx - 2]))
                if debug and T == 0:
                    fw.append((qG, f_dbg_qg))
            P.step("vector", lambda e, regw=regw, par=par: e.tensor_tensor(
                out=f_sb[par][:], in0=regw, in1=s3B[:], op=ALU.mult),
                waits=fw, incs=[(dv, 1)])
            P.step("vector", lambda e: e.tensor_copy(out=st_sm[0:1, 14:15], in_=st_sm[0:1, 14:15]),
                   incs=[(dv, 1)])
            dv_f = dv.v
            if sI == 3:
                psum_free_dv = dv_f
            if debug and T == 0:
                P.step("gpsimd", lambda e, sI=sI, par=par: e.dma_start(
                    out=dbg["F0"][:, sI * 1024:(sI + 1) * 1024], in_=f_sb[par][:]),
                    waits=[(dv, dv_f)], incs=[(qG, 16)])
                f_dbg_qg = qG.v
            lw = [(dv, h_dvoutfin[gidx - 2])] if gidx >= 2 else [(dv, h_dvy1[14 + gidx])]
            P.step("sync", lambda e, t0=t0, sI=sI, par=par: e.dma_start(
                out=xt[par][:], in_=y1d[t0 + sI * 128:t0 + (sI + 1) * 128, :]),
                waits=lw, incs=[(qS, 16)])
            qs_y1 = qS.v
            c = tuple(st_sm[:, k + 8 * par:k + 8 * par + 1] for k in range(4))
            dv_fin3, _ = emit_ln(f_sb[par][:], sqt[par][:], c,
                                 sq_waits=[(dv, dv_f)], stats_waits=[])
            tw = [(dv, dv_fin3)]
            if gidx >= 1:
                tw.append((dv, h_dvoutfin[gidx - 1]))
            P.step("scalar", lambda e, par=par, c=c: e.activation(
                out=t_sb[:], in_=f_sb[par][:], func=AF.Identity, bias=c[3], scale=c[2]),
                waits=tw, incs=[(se, 1)])
            P.step("scalar", lambda e: e.activation(out=st_sm[0:1, 15:16],
                                                    in_=st_sm[0:1, 15:16], func=AF.Copy),
                   incs=[(se, 1)])
            h_se_t3.append(se.v)
            if debug and T == 0:
                P.step("gpsimd", lambda e, sI=sI: e.dma_start(
                    out=dbg["T0"][:, sI * 1024:(sI + 1) * 1024], in_=t_sb[:]),
                    waits=[(se, se.v)], incs=[(qG, 16)])
                P.step("gpsimd", lambda e, sI=sI, par=par: e.dma_start(
                    out=dbg["X0"][:, sI * 1024:(sI + 1) * 1024], in_=xt[par][:]),
                    waits=[(qS, qs_y1)], incs=[(qG, 16)])
                t_dbg_qg = qG.v
            ow = [(se, se.v), (qS, qs_y1)]
            if debug and T == 0:
                ow.append((qG, t_dbg_qg))
            if gidx >= 2:
                ow.append((st_out, 16 * (gidx - 1)))
            P.step("vector", lambda e, par=par: e.tensor_add(
                out=out_sb[par][:], in0=t_sb[:], in1=xt[par][:]),
                waits=ow, incs=[(dv, 1)])
            P.step("vector", lambda e: e.tensor_copy(out=st_sm[0:1, 14:15], in_=st_sm[0:1, 14:15]),
                   incs=[(dv, 1)])
            h_dvoutfin.append(dv.v)
            P.step("gpsimd", lambda e, t0=t0, sI=sI, par=par: e.dma_start(
                out=out[t0 + sI * 128:t0 + (sI + 1) * 128, :], in_=out_sb[par][:]),
                waits=[(dv, dv.v)], incs=[(st_out, 16)])
            h_qout.append(st_out.v)
            n_out += 1
            gidx += 1
        prev_f_dv = psum_free_dv

    P.step("gpsimd", lambda e: e.engine_nop(), waits=[(st_out, 16 * n_out)])

    with es:
        block = es.enter_context(nc.Block())
        P.emit(block)
    return nc


def _tiles_for_core(c):
    cp = c % 4
    return [4 * b + ((cp + b) % 4) for b in range(NL)]


def _stage(inputs):
    x = np.ascontiguousarray(np.asarray(inputs["L0_attn_x"], dtype=np.float32))
    w = {k: np.ascontiguousarray(np.asarray(v, dtype=np.float32)) for k, v in inputs.items()
         if k != "L0_attn_x"}
    shared = dict(
        wq=w["L0_attn_WQ"], wk=w["L0_attn_WK"], wv=w["L0_attn_WV"], wo=w["L0_attn_WO"],
        w1t=np.ascontiguousarray(w["L0_ffn_w1_w"].T),
        w2t=np.ascontiguousarray(w["L0_ffn_w2_w"].T),
        h1t=np.ascontiguousarray(w["L0_ffn_h1_w"].T),
        h2t=np.ascontiguousarray(w["L0_ffn_h2_w"].T),
        w3t=np.ascontiguousarray(w["L0_ffn_w3_w"].T),
        s1c=np.ascontiguousarray(w["L0_ffn_w1_s"].reshape(32, 128).T),
        s2c=np.ascontiguousarray(w["L0_ffn_w2_s"].reshape(32, 128).T),
        hs1c=np.ascontiguousarray(w["L0_ffn_h1_s"].reshape(32, 128).T),
        hs2c=np.ascontiguousarray(w["L0_ffn_h2_s"].reshape(32, 128).T),
        s3r=np.ascontiguousarray(w["L0_ffn_w3_s"].reshape(1, D)),
        identb=np.eye(128, dtype=ml_dtypes.bfloat16),
        ones_c1=np.ones((128, 1), dtype=ml_dtypes.bfloat16),
        ones_r128=np.ones((1, 128), dtype=ml_dtypes.bfloat16),
        ones_r64f=np.ones((1, 64), dtype=np.float32),
        ones_r128f=np.ones((1, 128), dtype=np.float32),
        epsb=np.full((128, 1), EPS, dtype=np.float32),
    )
    maps = []
    pidx = np.arange(128)[:, None]
    qidx = np.arange(128)[None, :]
    for c in range(8):
        g = c // 4
        tiles = _tiles_for_core(c)
        xo = np.concatenate([x[g, t * 128:(t + 1) * 128, :] for t in tiles], axis=0)
        maskT = np.zeros((128, NL * 4 * 128), dtype=np.float32)
        for b in range(NL):
            tb = tiles[b]
            for sblk in range(4):
                kglob = 512 * b + 128 * sblk + pidx
                qglob = tb * 128 + qidx
                maskT[:, (b * 4 + sblk) * 128:(b * 4 + sblk + 1) * 128] = (kglob <= qglob)
        m = dict(shared)
        m["xk"] = np.ascontiguousarray(x[g])
        m["xo"] = np.ascontiguousarray(xo)
        m["maskT"] = maskT.astype(ml_dtypes.bfloat16)
        maps.append(m)
    return maps


def kernel(**inputs):
    maps = _stage(inputs)
    nc = build_nc(debug=False)
    res = run_bass_kernel_spmd(nc, maps, core_ids=list(range(8)))
    y = np.empty((BATCH, S, D), dtype=np.float32)
    for c in range(8):
        g = c // 4
        o = np.asarray(res.results[c]["out"], dtype=np.float32)
        for b, t in enumerate(_tiles_for_core(c)):
            y[g, t * 128:(t + 1) * 128, :] = o[b * 128:(b + 1) * 128, :]
    return y
